# version 07
# baseline (speedup 1.0000x reference)
import math
from contextlib import ExitStack

import numpy as np
import ml_dtypes

import concourse.bass as bass
import concourse.mybir as mybir
from concourse.bass_utils import run_bass_kernel_spmd

F32 = mybir.dt.float32
BF16 = mybir.dt.bfloat16
F32R = mybir.dt.float32r
AF = mybir.ActivationFunctionType
ALU = mybir.AluOpType
AX = mybir.AxisListType

T = 2048
E = 1024
FF = 2048
NEXP = 8
LAMBDA_INIT = 0.8 - 0.6 * math.exp(-0.3)
LOG_1ML = math.log(1.0 - LAMBDA_INIT)


def _rope_tables():
    D = 64
    inv = 1.0 / 10000 ** (np.arange(0, D, 2) / D)
    pos = np.arange(T, dtype=np.float64)
    fr = pos[:, None] * inv[None, :]
    sv = (np.arange(0, D, 2) + 0.4 * D) / (1.4 * D)
    power = (pos - T // 2) / 512.0
    scl = sv[None, :] ** power[:, None]
    i32 = np.arange(32)
    se = scl[:, (2 * i32) % 32]
    so = scl[:, (2 * i32 + 1) % 32]
    cos, sin = np.cos(fr), np.sin(fr)
    Cq = np.concatenate([(cos * se).T, (cos * so).T] * 2, 0)
    Sq = np.concatenate([(-sin * se).T, (sin * so).T] * 2, 0)
    sei, soi = 1.0 / se, 1.0 / so
    Ck = np.concatenate([(cos * sei).T, (cos * soi).T] * 2, 0)
    Sk = np.concatenate([(-sin * sei).T, (sin * soi).T] * 2, 0)
    s = 64.0 ** -0.5
    bfc = lambda a: np.ascontiguousarray(a.astype(np.float32), dtype=ml_dtypes.bfloat16)
    return bfc(Cq * s), bfc(Sq * s), bfc(Ck), bfc(Sk)


_PERM64 = np.concatenate([np.arange(0, 64, 2), np.arange(1, 64, 2)])
_PERM_E = np.concatenate([h * 64 + _PERM64 for h in range(16)])


class Emit:
    ENGS = ("tensor", "vector", "scalar", "sync", "gpsimd")
    _ESEM = {"tensor": "pe", "vector": "ve", "scalar": "se", "gpsimd": "gp",
             "dsync": "dS", "dgp": "dG"}

    def __init__(self, nc):
        self.nc = nc
        self.cmds = {e: [] for e in self.ENGS}
        self.cnt = {e: 0 for e in self.ENGS}
        self.dma_cnt = {"dsync": 0, "dgp": 0}
        self.sems = {}
        self.watermark = {}

    def make_sems(self, ctx):
        for name in ("pe", "ve", "se", "gp", "dS", "dG"):
            self.sems[name] = ctx.enter_context(self.nc.semaphore(name))

    def wait(self, waiter, token):
        if token is None:
            return
        prod, count = token
        if count <= 0:
            return
        sem_name = self._ESEM[prod]
        val = count * 16 if prod in ("dsync", "dgp") else count
        key = (waiter, sem_name)
        if self.watermark.get(key, 0) >= val:
            return
        self.watermark[key] = val
        sem = self.sems[sem_name]
        self.cmds[waiter].append(lambda eng, s=sem, v=val: eng.wait_ge(s, v))

    def vsafe(self):
        self.wait("vector", self.now("vector"))

    def wait_deferred(self, waiter, cell):
        def w(eng, c=cell):
            prod, count = c["tok"]
            val = count * 16 if prod in ("dsync", "dgp") else count
            return eng.wait_ge(self.sems[self._ESEM[prod]], val)
        self.cmds[waiter].append(w)

    def op(self, eng_name, fn):
        sem = self.sems[self._ESEM[eng_name]]
        self.cmds[eng_name].append(lambda eng, f=fn, s=sem: f(eng).then_inc(s, 1))
        self.cnt[eng_name] += 1
        return (eng_name, self.cnt[eng_name])

    def dma(self, queue, fn):
        eng = {"dsync": "sync", "dgp": "gpsimd"}[queue]
        sem = self.sems[self._ESEM[queue]]
        self.cmds[eng].append(lambda e, f=fn, s=sem: f(e).then_inc(s, 16))
        self.dma_cnt[queue] += 1
        return (queue, self.dma_cnt[queue])

    def now(self, eng_name):
        if eng_name in ("dsync", "dgp"):
            return (eng_name, self.dma_cnt[eng_name])
        return (eng_name, self.cnt[eng_name])

    def fence(self):
        toks = [self.now(e) for e in ("tensor", "vector", "scalar", "gpsimd")]
        toks += [self.now("dsync"), self.now("dgp")]
        for w in ("tensor", "vector", "scalar", "sync", "gpsimd"):
            for t in toks:
                if (w, t[0]) in (("sync", "dsync"), ("gpsimd", "dgp")):
                    continue
                if t[0] == w:
                    continue
                self.wait(w, t)

    def final_wait(self):
        self.wait("sync", ("dsync", self.dma_cnt["dsync"]))
        self.wait("sync", ("dgp", self.dma_cnt["dgp"]))

    def replay(self, block):
        for name in self.ENGS:
            sec = getattr(block, name)

            def runner(eng, _cmds=self.cmds[name]):
                for c in _cmds:
                    c(eng)
            sec(runner)


_QMAX = [10**9]


def build_program(dbg=(), stop_after=99):
    nc = bass.Bass("TRN2", target_bir_lowering=False)

    def din(nm, shape, dt=F32):
        return nc.dram_tensor(nm, shape, dt, kind="ExternalInput").ap()

    d_x = din("x", [T, E])
    d_wqT = din("wqT", [E, E]); d_wkT = din("wkT", [E, E])
    d_wvT = din("wvT", [E, E]); d_woT = din("woT", [E, E])
    d_wmod = din("wmod", [E, 6 * E])
    d_c8 = din("c8", [128, 8])
    d_lam = din("lamv", [1, 256])
    d_rwT = din("rwT", [E, 8])
    d_gam = din("gam", [128, 8]); d_bet = din("bet", [128, 8])
    d_sw1 = din("sw1", [FF, E]); d_sw2 = din("sw2", [E, FF])
    d_ew1 = din("ew1", [NEXP, FF, E]); d_ew2 = din("ew2", [NEXP, E, FF])
    d_idb = din("idb", [128, 128], BF16)
    d_idf = din("idf", [128, 128], F32)
    d_Cq = din("Cq", [128, T], BF16); d_Sq = din("Sq", [128, T], BF16)
    d_Ck = din("Ck", [128, T], BF16); d_Sk = din("Sk", [128, T], BF16)
    d_out = nc.dram_tensor("out", [T, E], F32, kind="ExternalOutput").ap()
    d_q1 = nc.dram_tensor("q1scr", [9, FF, E], BF16).ap()
    d_q2 = nc.dram_tensor("q2scr", [9, E, FF], BF16).ap()

    dbg_specs = {
        "dbg_mod": ([1, 6 * E], F32), "dbg_haT": ([128, 8, T], BF16),
        "dbg_qT": ([128, 8, T], BF16), "dbg_kT": ([128, 8, T], BF16),
        "dbg_v": ([128, 16, E], BF16), "dbg_attnT": ([4, 128, 8, 512], BF16),
        "dbg_x2": ([128, 16, E], BF16), "dbg_hiT": ([128, 8, T], BF16),
        "dbg_gate": ([128, 16, 8], F32), "dbg_w1q": ([128, 8, FF], BF16),
        "dbg_w2q": ([128, 16, E], BF16),
    }
    dbg_outs = {nm: nc.dram_tensor(nm, *dbg_specs[nm], kind="ExternalOutput").ap()
                for nm in dbg}

    ctx = ExitStack()
    em = Emit(nc)
    em.make_sems(ctx)

    def sbuf(nm, shape, dt=F32):
        cm = nc.sbuf_tensor(nm, shape, dt)
        return cm.__enter__(), cm

    x_res, _ = sbuf("x_res", [128, 16, E], BF16)
    idf, _ = sbuf("idf_sb", [128, 128], F32)
    ones_c, _ = sbuf("ones_c", [1, 128], BF16)
    ones_l, _ = sbuf("ones_l", [128, 1], BF16)
    grow, _ = sbuf("grow", [1, 512], F32)
    epsc, _ = sbuf("epsc", [1, 1], F32)
    lgc, _ = sbuf("lgc", [1, 1], F32)
    growbf, _ = sbuf("growbf", [1, 512], BF16)
    gaB, _ = sbuf("gaB", [128, E], BF16)
    gbB, _ = sbuf("gbB", [128, E], BF16)
    cb1B, _ = sbuf("cb1B", [128, E], BF16)
    sbB, _ = sbuf("sbB", [128, E], BF16)
    ca1c, _ = sbuf("ca1c", [128, 8], F32)
    sac, _ = sbuf("sac", [128, 8], F32)
    gamc, _ = sbuf("gamc", [128, 8], F32)
    betc, _ = sbuf("betc", [128, 8], F32)
    c8, _ = sbuf("c8_sb", [128, 8], F32)
    lamrows, _ = sbuf("lamrows", [1, 256], F32)
    lamneg, _ = sbuf("lamneg", [128, 1], F32)
    gates, _ = sbuf("gates", [128, 16, 8], F32)
    stat, _ = sbuf("stat", [128, 32], F32)
    mg, _ = sbuf("mgcols", [128, 8], F32)
    pbw = [nc.psum_tensor(f"bankw{j}", [128, 2048], F32).__enter__() for j in range(2)]
    pb = [pbw[j // 4][:, (j % 4) * 512:(j % 4 + 1) * 512] for j in range(8)]


    def _early_finish():
        ostE, _ = sbuf("ostE", [128, 2, E], F32)
        em.fence()
        t_prev = {}
        for i2 in range(16):
            sl2 = i2 % 2
            if i2 >= 2:
                em.wait("vector", t_prev[i2 - 2])
            t_f = em.op("vector", lambda v, i_=i2, s_=sl2: v.tensor_copy(ostE[:, s_, :], x_res[:, i_, :]))
            em.wait("sync", t_f)
            t_prev[i2] = em.dma("dsync", lambda s, i_=i2, s_=sl2: s.dma_start(
                d_out[i_ * 128:(i_ + 1) * 128, :], ostE[:, s_, :]))
        em.final_wait()
        with nc.Block() as blk:
            em.replay(blk)
        ctx.close()
        return nc

    modrow, modrow_cm = sbuf("modrow", [1, 6 * E], F32)
    rwk, rwk_cm = sbuf("rwk", [1, 4 * E], F32)
    rbf, rbf_cm = sbuf("rbf", [1, 4 * E], BF16)
    for dst, src in [(idf, d_idf), (c8, d_c8), (lamrows, d_lam),
                     (gamc, d_gam), (betc, d_bet)]:
        t_ld0 = em.dma("dsync", lambda s, d=dst, a=src: s.dma_start(d[:], a))
    if stop_after == -1:
        return _early_finish()
    em.wait("vector", t_ld0)
    em.op("vector", lambda v: v.memset(ones_c[:], 1.0))
    em.op("vector", lambda v: v.memset(ones_l[:], 1.0))
    em.op("vector", lambda v: v.memset(epsc[:], 1e-5))
    em.op("vector", lambda v: v.memset(lgc[:], LOG_1ML))

    em.op("vector", lambda v: v.tensor_tensor(rwk[0:1, 0:64], lamrows[0:1, 0:64], lamrows[0:1, 64:128], ALU.mult))
    em.op("vector", lambda v: v.tensor_tensor(rwk[0:1, 64:128], lamrows[0:1, 128:192], lamrows[0:1, 192:256], ALU.mult))
    em.vsafe()
    em.op("vector", lambda v: v.reduce_sum(rwk[0:1, 128:129], rwk[0:1, 0:64], AX.X))
    t = em.op("vector", lambda v: v.reduce_sum(rwk[0:1, 129:130], rwk[0:1, 64:128], AX.X))
    em.wait("scalar", t)
    t = em.op("scalar", lambda s: s.activation(rwk[0:1, 130:132], rwk[0:1, 128:130], AF.Exp))
    em.wait("vector", t)
    em.op("vector", lambda v: v.tensor_tensor(rwk[0:1, 132:133], rwk[0:1, 131:132], rwk[0:1, 130:131], ALU.subtract))
    em.vsafe()
    em.op("vector", lambda v: v.tensor_scalar(rwk[0:1, 132:133], rwk[0:1, 132:133], LAMBDA_INIT, None, ALU.subtract, ALU.bypass))
    em.vsafe()
    t = em.op("vector", lambda v: v.tensor_copy(rbf[0:1, 0:1], rwk[0:1, 132:133]))
    em.wait("tensor", t)
    t = em.op("tensor", lambda te: te.matmul(pb[7][:, 0:1], ones_c[:], rbf[0:1, 0:1], start=True, stop=True))
    em.wait("vector", t)
    t_lamneg = em.op("vector", lambda v: v.tensor_copy(lamneg[:], pb[7][:, 0:1]))
    if stop_after == 10:
        return _early_finish()

    scf, scf_cm = sbuf("scf", [128, 8], F32)
    schi, schi_cm = sbuf("schi", [128, 8], BF16)
    sclo, sclo_cm = sbuf("sclo", [128, 8], BF16)
    em.wait("scalar", em.now("dsync"))
    t_sf = em.op("scalar", lambda s: s.activation(scf[:], c8[:], AF.Silu))
    em.wait("vector", t_sf)
    em.op("vector", lambda v: v.tensor_copy(schi[:], scf[:]))
    em.vsafe()
    t_sc = em.op("vector", lambda v: v.tensor_tensor(sclo[:], scf[:], schi[:], ALU.subtract))

    wmf, wmf_cm = sbuf("wmf", [128, 6 * E], F32)
    wmhi, wmhi_cm = sbuf("wmhi", [128, 6 * E], BF16)
    wmlo, wmlo_cm = sbuf("wmlo", [128, 6 * E], BF16)
    t_modz = em.op("vector", lambda v: v.memset(modrow[:], 0.0))
    t_ev = {n: t_modz for n in range(12)}
    t_cp_prev = None
    t_mm_last_prev = None
    em.wait("tensor", t_sc)
    for kk in range(8):
        if t_cp_prev is not None:
            em.wait("sync", t_cp_prev)
        t_ld = em.dma("dsync", lambda s, k=kk: s.dma_start(wmf[:], d_wmod[k * 128:(k + 1) * 128, :]))
        em.wait("vector", t_ld)
        if t_mm_last_prev is not None:
            em.wait("vector", t_mm_last_prev)
        em.op("vector", lambda v: v.tensor_copy(wmhi[:], wmf[:]))
        t_cp = em.op("vector", lambda v: v.tensor_tensor(wmlo[:], wmf[:], wmhi[:], ALU.subtract))
        t_cp_prev = t_cp
        em.wait("tensor", t_cp)
        for n in range(12):
            bank = pb[n % 4]
            em.wait("tensor", t_ev[n])
            em.op("tensor", lambda te, b=bank, k=kk, n_=n: te.matmul(
                b[0:1, :], schi[:, k:k + 1], wmhi[:, n_ * 512:(n_ + 1) * 512], start=True, stop=False))
            em.op("tensor", lambda te, b=bank, k=kk, n_=n: te.matmul(
                b[0:1, :], schi[:, k:k + 1], wmlo[:, n_ * 512:(n_ + 1) * 512], start=False, stop=False))
            t_mm = em.op("tensor", lambda te, b=bank, k=kk, n_=n: te.matmul(
                b[0:1, :], sclo[:, k:k + 1], wmhi[:, n_ * 512:(n_ + 1) * 512], start=False, stop=True))
            em.wait("vector", t_mm)
            t_ev[n] = em.op("vector", lambda v, b=bank, n_=n: v.tensor_tensor(
                modrow[0:1, n_ * 512:(n_ + 1) * 512], modrow[0:1, n_ * 512:(n_ + 1) * 512],
                b[0:1, :], ALU.add))
        t_mm_last_prev = t_mm
    t_mod = max(t_ev.values(), key=lambda x: x[1])
    if stop_after == 11:
        return _early_finish()
    wmlo_cm.__exit__(None, None, None)
    wmhi_cm.__exit__(None, None, None)
    wmf_cm.__exit__(None, None, None)
    sclo_cm.__exit__(None, None, None)
    schi_cm.__exit__(None, None, None)
    scf_cm.__exit__(None, None, None)

    if "dbg_mod" in dbg:
        em.wait("sync", t_mod)
        em.dma("dsync", lambda s: s.dma_start(dbg_outs["dbg_mod"], modrow[:]))

    SA, CA, GA, SB, CB, GB = range(6)

    def row(i, lo=0, hi=E):
        return modrow[0:1, i * E + lo:i * E + hi]

    for gi, g_idx in enumerate([GA, GB]):
        hi_ = rwk[0:1, 0:E]; lo_ = rwk[0:1, E:2 * E]
        d_ = rwk[0:1, 2 * E:3 * E]; m_ = rwk[0:1, 3 * E:4 * E]
        em.op("vector", lambda v, g=g_idx, o=hi_: v.tensor_scalar(o, row(g), 0.25, None, ALU.max, ALU.bypass))
        em.op("vector", lambda v, g=g_idx, o=lo_: v.tensor_scalar(o, row(g), -0.25, None, ALU.min, ALU.bypass))
        em.op("vector", lambda v, g=g_idx, o=m_: v.tensor_scalar(o, row(g), 0.0, 1.0, ALU.is_ge, ALU.mult))
        em.vsafe()
        em.op("vector", lambda v, a=hi_, b=lo_, o=d_: v.tensor_tensor(o, a, b, ALU.subtract))
        em.vsafe()
        em.op("vector", lambda v, a=d_, b=m_, o=d_: v.tensor_tensor(o, a, b, ALU.mult))
        em.vsafe()
        em.op("vector", lambda v, a=lo_, b=d_, gi_=gi: v.tensor_tensor(
            rbf[0:1, gi_ * E:(gi_ + 1) * E], a, b, ALU.add))
    em.op("vector", lambda v: v.tensor_scalar(rbf[0:1, 2 * E:3 * E], row(CB), 1.0, None, ALU.add, ALU.bypass))
    t_rows = em.op("vector", lambda v: v.tensor_copy(rbf[0:1, 3 * E:4 * E], row(SB)))

    em.wait("tensor", t_rows)
    for dst, off in [(gaB, 0), (gbB, E), (cb1B, 2 * E), (sbB, 3 * E)]:
        for h in range(2):
            b = pb[4 + h]
            tm = em.op("tensor", lambda te, b_=b, o=off, h_=h: te.matmul(
                b_[:, :], ones_c[:], rbf[0:1, o + h_ * 512:o + (h_ + 1) * 512], start=True, stop=True))
            em.wait("scalar", tm)
            tc = em.op("scalar", lambda s, b_=b, d=dst, h_=h: s.copy(d[:, h_ * 512:(h_ + 1) * 512], b_[:, :]))
            em.wait("tensor", tc)

    if stop_after == 13:
        return _early_finish()
    t_ca = em.op("vector", lambda v: v.tensor_scalar(rwk[0:1, 0:E], row(CA), 1.0, None, ALU.add, ALU.bypass))
    em.wait("tensor", t_ca)
    em.wait("tensor", t_lamneg)
    t_col_ev = None
    for dst, src_fn in [(ca1c, lambda j: rwk[0:1, j * 128:(j + 1) * 128]),
                        (sac, lambda j: row(SA, j * 128, (j + 1) * 128))]:
        for j in range(8):
            b = pb[6 + (j % 2)]
            if t_col_ev is not None:
                em.wait("tensor", t_col_ev)
            tm = em.op("tensor", lambda te, b_=b, s_=src_fn(j): te.transpose(
                b_[:, 0:1], s_, idf[0:1, 0:1]))
            em.wait("vector", tm)
            t_col_ev = em.op("vector", lambda v, b_=b, d=dst, j_=j: v.tensor_copy(d[:, j_:j_ + 1], b_[:, 0:1]))

    rbf_cm.__exit__(None, None, None)
    rwk_cm.__exit__(None, None, None)
    modrow_cm.__exit__(None, None, None)

    em.fence()
    if stop_after == 0:
        return _early_finish()

    wst, wst_cm = sbuf("wst", [128, 2, E], F32)
    geb, geb_cm = sbuf("geb", [128, E], BF16)
    tqb, tqb_cm = sbuf("tqb", [128, 2, E], BF16)
    qT, qT_cm = sbuf("qT", [128, 8, T], BF16)
    kT, kT_cm = sbuf("kT", [128, 8, T], BF16)
    vtok, vtok_cm = sbuf("vtok", [128, 16, E], BF16)
    haT, haT_cm = sbuf("haT", [128, 8, T], BF16)
    xstg, xstg_cm = sbuf("xstg", [128, 2, E], F32)
    zbuf, zbuf_cm = sbuf("zbuf", [128, 2, E], F32)
    junk1, junk1_cm = sbuf("junk1", [128, E], BF16)

    t_xuse = {}
    t_trans = {}
    t_evac = {}
    for i in range(16):
        sl = i % 2
        if i >= 2:
            em.wait("sync", t_xuse[i - 2])
        t_xld = em.dma("dsync", lambda s, i_=i, sl_=sl: s.dma_start(
            xstg[:, sl_, :], d_x[i_ * 128:(i_ + 1) * 128, :]))
        em.wait("vector", t_xld)
        em.op("vector", lambda v, sl_=sl: v.reduce_sum(stat[:, 3:4], xstg[:, sl_, :], AX.X))
        em.op("vector", lambda v, sl_=sl: v.scalar_tensor_tensor(
            junk1[:, :], xstg[:, sl_, :], 1.0, xstg[:, sl_, :], ALU.mult, ALU.mult, accum_out=stat[:, 4:5]))
        em.vsafe()
        em.op("vector", lambda v: v.tensor_scalar(stat[:, 0:1], stat[:, 3:4], 1.0 / E, None, ALU.mult, ALU.bypass))
        em.vsafe()
        em.op("vector", lambda v: v.tensor_tensor(stat[:, 6:7], stat[:, 0:1], stat[:, 0:1], ALU.mult))
        em.op("vector", lambda v: v.tensor_scalar(stat[:, 5:6], stat[:, 4:5], 1.0 / E, None, ALU.mult, ALU.bypass))
        em.vsafe()
        t = em.op("vector", lambda v: v.tensor_scalar(stat[:, 1:2], stat[:, 5:6], stat[:, 6:7], 1e-5, ALU.subtract, ALU.add))
        em.wait("scalar", t)
        em.op("scalar", lambda s: s.activation(stat[:, 2:3], stat[:, 1:2], AF.Ln))
        t = em.op("scalar", lambda s: s.activation(stat[:, 2:3], stat[:, 2:3], AF.Exp, scale=-0.5))
        em.wait("vector", t)
        if i >= 2:
            em.wait("vector", t_trans[(i - 2, 7)])
        t_z = em.op("vector", lambda v, sl_=sl: v.tensor_scalar(
            zbuf[:, sl_, :], xstg[:, sl_, :], stat[:, 0:1], stat[:, 2:3], ALU.subtract, ALU.mult))
        t_xuse[i] = em.op("vector", lambda v, i_=i, sl_=sl: v.tensor_copy(x_res[:, i_, :], xstg[:, sl_, :]))
        em.wait("tensor", t_z)
        for j in range(8):
            if i:
                em.wait("tensor", t_evac[(i - 1, j)])
            t_trans[(i, j)] = em.op("tensor", lambda te, sl_=sl, j_=j: te.transpose(
                pb[j_][:, 0:128], zbuf[:, sl_, j_ * 128:(j_ + 1) * 128], idf[:]))
        for j in range(8):
            em.wait("vector", t_trans[(i, j)])
            t_evac[(i, j)] = em.op("vector", lambda v, i_=i, j_=j: v.tensor_scalar(
                haT[:, j_, i_ * 128:(i_ + 1) * 128], pb[j_][:, 0:128],
                ca1c[:, j_:j_ + 1], sac[:, j_:j_ + 1], ALU.mult, ALU.add))
    t_haT = em.now("vector")
    junk1_cm.__exit__(None, None, None)
    zbuf_cm.__exit__(None, None, None)
    xstg_cm.__exit__(None, None, None)

    if "dbg_haT" in dbg:
        em.wait("sync", t_haT)
        em.dma("dsync", lambda s: s.dma_start(dbg_outs["dbg_haT"], haT[:]))

    em.fence()
    if stop_after == 1:
        return _early_finish()

    wbuf, wbuf_cm = sbuf("wbuf", [128, 8, 512], BF16)
    ctab, ctab_cm = sbuf("ctab", [128, T], BF16)
    stab, stab_cm = sbuf("stab", [128, T], BF16)
    rt1, rt1_cm = sbuf("rt1", [128, 512], F32)
    rt2, rt2_cm = sbuf("rt2", [128, 512], F32)

    def proj_qk(dst, d_w, d_C, d_S):
        em.wait("sync", em.now("vector"))
        t_c = em.dma("dsync", lambda s, a=d_C: s.dma_start(ctab[:], a))
        t_s = em.dma("dsync", lambda s, a=d_S: s.dma_start(stab[:], a))
        em.wait("vector", t_s)
        t_rope = {}
        t_w = None
        for mn in range(32):
            m, n = divmod(mn, 4)
            if m % 4 == 0 and n == 0:
                em.wait("gpsimd", em.now("tensor"))
                t_w = em.dma("dgp", lambda g, w=d_w, m_=m: g.dma_start(
                    wbuf[:], w[:, m_ * 128:m_ * 128 + 512].rearrange("(k p) e -> p k e", p=128)))
                em.wait("tensor", t_w)
            bank = pb[mn % 4]
            em.wait("tensor", t_rope.get(mn - 4))
            for kk in range(8):
                t_mm = em.op("tensor", lambda te, b=bank, m_=m, n_=n, k=kk: te.matmul(
                    b[:, :], wbuf[:, k, (m_ % 4) * 128:(m_ % 4 + 1) * 128],
                    haT[:, k, n_ * 512:(n_ + 1) * 512], start=(k == 0), stop=(k == 7)))
            em.wait("vector", t_mm)
            ts_ = slice(n * 512, (n + 1) * 512)
            em.op("vector", lambda v, b=bank, t_=ts_: v.tensor_tensor(rt1[:, :], b[:, :], ctab[:, t_], ALU.mult))
            for l in (0, 64):
                em.op("vector", lambda v, b=bank, t_=ts_, l_=l: v.tensor_tensor(
                    rt2[l_:l_ + 32, :], b[l_ + 32:l_ + 64, :], stab[l_:l_ + 32, t_], ALU.mult))
                em.op("vector", lambda v, b=bank, t_=ts_, l_=l: v.tensor_tensor(
                    rt2[l_ + 32:l_ + 64, :], b[l_:l_ + 32, :], stab[l_ + 32:l_ + 64, t_], ALU.mult))
            t_rope[mn] = em.op("vector", lambda v, d=dst, m_=m, t_=ts_: v.tensor_tensor(
                d[:, m_, t_], rt1[:, :], rt2[:, :], ALU.add))

    proj_qk(qT, d_wqT, d_Cq, d_Sq)
    proj_qk(kT, d_wkT, d_Ck, d_Sk)

    t_vev = {}
    for n in range(2):
        em.wait("gpsimd", em.now("tensor"))
        t_wv = em.dma("dgp", lambda g, n_=n: g.dma_start(
            wbuf[:], d_wvT[:, n_ * 512:(n_ + 1) * 512].rearrange("(k p) e -> p k e", p=128)))
        em.wait("tensor", t_wv)
        for m in range(16):
            mn = n * 16 + m
            bank = pb[4 + mn % 2]
            em.wait("tensor", t_vev.get(mn - 2))
            for kk in range(8):
                t_mm = em.op("tensor", lambda te, b=bank, m_=m, k=kk: te.matmul(
                    b[:, :], haT[:, k, m_ * 128:(m_ + 1) * 128],
                    wbuf[:, k, :], start=(k == 0), stop=(k == 7)))
            em.wait("scalar", t_mm)
            t_vev[mn] = em.op("scalar", lambda s, b=bank, m_=m, n_=n: s.copy(
                vtok[:, m_, n_ * 512:(n_ + 1) * 512], b[:, :]))

    if "dbg_qT" in dbg:
        em.fence()
        em.dma("dsync", lambda s: s.dma_start(dbg_outs["dbg_qT"], qT[:]))
        em.dma("dsync", lambda s: s.dma_start(dbg_outs["dbg_kT"], kT[:]))
    if "dbg_v" in dbg:
        em.fence()
        em.dma("dsync", lambda s: s.dma_start(dbg_outs["dbg_v"], vtok[:]))

    rt2_cm.__exit__(None, None, None)
    rt1_cm.__exit__(None, None, None)
    stab_cm.__exit__(None, None, None)
    ctab_cm.__exit__(None, None, None)
    wbuf_cm.__exit__(None, None, None)
    haT_cm.__exit__(None, None, None)

    em.fence()
    if stop_after == 2:
        return _early_finish()

    quant_jobs = []
    t_qst1 = {}
    t_qst2 = {}
    _qslot = [0]
    _qprev = {}

    QB = {"wst": wst, "tqb": tqb, "slots": 2, "prefetch": False}
    _wfree_ve = {}
    _wfree_se = {}

    def _quant_unit(srcap, dstap, cols, done_cb, half2_src=None, half2_dst=None):
        def job():
            wst_, tqb_, S = QB["wst"], QB["tqb"], QB["slots"]
            sl = _qslot[0] % S
            _qslot[0] += 1
            em.wait("gpsimd", _wfree_ve.get((S, sl)))
            em.wait("gpsimd", _wfree_se.get((S, sl)))
            t_ld = em.dma("dgp", lambda g: g.dma_start(wst_[:, sl, :], srcap))
            sl2 = None
            if half2_src is not None:
                sl2 = _qslot[0] % S
                _qslot[0] += 1
                em.wait("gpsimd", _wfree_ve.get((S, sl2)))
                em.wait("gpsimd", _wfree_se.get((S, sl2)))
                t_ld2 = em.dma("dgp", lambda g: g.dma_start(wst_[:, sl2, :], half2_src))
            yield
            em.wait("scalar", t_ld)
            em.wait("scalar", _qprev.get((S, sl)))
            t_abs = em.op("scalar", lambda s, s1=sl: s.activation(
                tqb_[:, s1, :], wst_[:, s1, :], AF.Abs, accum_out=stat[:, 22:23]))
            _wfree_se[(S, sl)] = t_abs
            if sl2 is not None:
                em.wait("scalar", t_ld2)
                em.wait("scalar", _qprev.get((S, sl2)))
                t_abs = em.op("scalar", lambda s, s1=sl2: s.activation(
                    tqb_[:, s1, :], wst_[:, s1, :], AF.Abs, accum_out=stat[:, 21:22]))
                _wfree_se[(S, sl2)] = t_abs
                em.wait("vector", t_abs)
                em.vsafe()
                em.op("vector", lambda v: v.tensor_tensor(stat[:, 22:23], stat[:, 22:23],
                                                          stat[:, 21:22], ALU.add))
            em.wait("vector", t_abs)
            em.vsafe()
            em.op("vector", lambda v: v.tensor_scalar(
                stat[:, 23:24], stat[:, 22:23], 1.0 / cols, 1e-5, ALU.mult, ALU.max))
            em.vsafe()
            em.op("vector", lambda v: v.tensor_scalar(stat[:, 26:27], stat[:, 23:24], 0.5, None, ALU.mult, ALU.bypass))
            em.vsafe()
            em.op("vector", lambda v: v.tensor_scalar(stat[:, 27:28], stat[:, 26:27], -1.0, None, ALU.mult, ALU.bypass))
            em.op("vector", lambda v: v.tensor_scalar(stat[:, 20:21], stat[:, 23:24], -1.0, None, ALU.mult, ALU.bypass))
            em.vsafe()
            t_st = None
            pairs = [(sl, dstap)] if half2_src is None else [(sl, dstap), (sl2, half2_dst)]
            for s_, d_ in pairs:
                em.op("vector", lambda v, s2=s_: v.tensor_scalar(
                    geb[:, :], wst_[:, s2, :], stat[:, 26:27], stat[:, 20:21], ALU.is_ge, ALU.mult))
                em.wait("vector", _qprev.get((S, s_)))
                em.op("vector", lambda v, s2=s_: v.tensor_scalar(
                    tqb_[:, s2, :], wst_[:, s2, :], stat[:, 27:28], stat[:, 20:21], ALU.is_le, ALU.mult))
                t_tq = em.op("vector", lambda v, s2=s_: v.tensor_tensor(
                    tqb_[:, s2, :], tqb_[:, s2, :], geb[:, :], ALU.subtract))
                _wfree_ve[(S, s_)] = t_tq
                em.wait("gpsimd", t_tq)
                t_st = em.dma("dgp", lambda g, s2=s_, dd=d_: g.dma_start(dd, tqb_[:, s2, :]))
                _qprev[(S, s_)] = t_st
            done_cb(t_st)
        job.nslots = 1 if half2_src is None else 2
        return job

    for mat9, (src1, src2) in enumerate(
            [(d_sw1, d_sw2)] + [(d_ew1[e], d_ew2[e]) for e in range(NEXP)]):
        for ch in range(16):
            quant_jobs.append(_quant_unit(
                src1[ch * 128:(ch + 1) * 128, :], d_q1[mat9, ch * 128:(ch + 1) * 128, :], E,
                lambda t, m=mat9: t_qst1.__setitem__(m, t)))
        for ch in range(8):
            quant_jobs.append(_quant_unit(
                src2[ch * 128:(ch + 1) * 128, 0:E], d_q2[mat9, ch * 128:(ch + 1) * 128, 0:E], FF,
                lambda t, m=mat9: t_qst2.__setitem__(m, t),
                half2_src=src2[ch * 128:(ch + 1) * 128, E:FF],
                half2_dst=d_q2[mat9, ch * 128:(ch + 1) * 128, E:FF]))
    quant_jobs.reverse()

    _qpop = [0]
    _primed = []
    _qfree = [2]

    def pop_quant(n=1):
        for _ in range(n):
            if not (quant_jobs or _primed):
                return
            _qpop[0] += 1
            if _qpop[0] > _QMAX[0]:
                if quant_jobs:
                    quant_jobs.pop()
                continue
            if QB["prefetch"]:
                while len(_primed) < 2 and quant_jobs and quant_jobs[-1].nslots <= _qfree[0]:
                    fn = quant_jobs.pop()
                    g = fn()
                    next(g)
                    _primed.append((g, fn.nslots))
                    _qfree[0] -= fn.nslots
                if _primed:
                    g, ns = _primed.pop(0)
                    for _x in g:
                        pass
                    _qfree[0] += ns
                elif quant_jobs:
                    for _x in quant_jobs.pop()():
                        pass
            else:
                if quant_jobs:
                    for _x in quant_jobs.pop()():
                        pass

    XE00, XE00_cm = sbuf("XE00", [128, T], BF16)
    XE10, XE10_cm = sbuf("XE10", [128, T], BF16)
    XE11, XE11_cm = sbuf("XE11", [128, T], BF16)
    XEp = [[XE00, XE00], [XE10, XE11]]
    Dbuf, Dbuf_cm = sbuf("Dbuf", [128, T], BF16)
    DTf, DTf_cm = sbuf("DTf", [128, 16, 4, 128], BF16)
    Abf, Abf_cm = sbuf("Abf", [128, 512], BF16)
    A2bf, A2bf_cm = sbuf("A2bf", [128, 512], BF16)
    gBsb, gBsb_cm = sbuf("gBsb", [128, 512], BF16)
    attnT, attnT_cm = sbuf("attnT", [128, 8, 512], BF16)
    wo8, wo8_cm = sbuf("wo8", [128, 8, 512], BF16)
    wotmp, wotmp_cm = sbuf("wotmp", [128, 512], BF16)

    t_sbank = {j: None for j in range(8)}
    t_dT = {}
    t_pv_of = {}
    pv_cells = {}
    t_attnT = {}
    t_wo_mm = {}
    st_prev = {"acopy": None, "s2": None, "d": None, "dT": None}

    t_expw = {}
    prev_n = [None]

    def emit_chunk_front(n):
        pg, i = divmod(n, 4)
        Q, p = divmod(pg, 8)
        tchunk = (pg // 8) * 4 + i
        par = n % 2
        mxc = 10 + 2 * par
        ngc = 8 if par == 0 else 14
        smc = 16 + 2 * par

        nm1 = prev_n[0]

        t_mm_h = {}
        for h in range(2):
            for c in range(4):
                bid = h * 4 + c
                em.wait("tensor", t_sbank[bid])
                base = 0 if h == 0 else 64
                tp = None if h == 0 else (64, 0)
                t_mm_h[h] = em.op("tensor", lambda te, b=pb[bid], p_=p, c_=c, tc=tchunk, ba=base, tp_=tp: te.matmul(
                    b[:, :], qT[ba:ba + 64, p_, tc * 128:(tc + 1) * 128],
                    kT[ba:ba + 64, p_, c_ * 512:(c_ + 1) * 512],
                    start=True, stop=True, tile_position=tp_))

        for h in range(2):
            em.wait("vector", t_mm_h[h])
            em.op("vector", lambda v, P=pbw[h], mc=mxc + h: v.reduce_max(
                stat[:, mc:mc + 1], P[:, :], AX.X))
        em.vsafe()
        t_ng = em.op("vector", lambda v, m0=mxc, n0=ngc: v.tensor_scalar(
            stat[:, n0:n0 + 2], stat[:, m0:m0 + 2], -1.0, None, ALU.mult, ALU.bypass))

        t_25 = None
        t_d1 = None
        if nm1 is not None:
            pm1 = nm1 % 2
            em.wait("vector", t_expw[(nm1, 1)])
            em.op("vector", lambda v, sc=16 + 2 * pm1: v.reciprocal(stat[:, 24:25], stat[:, sc:sc + 1]))
            em.op("vector", lambda v, sc=17 + 2 * pm1: v.reciprocal(stat[:, 30:31], stat[:, sc:sc + 1]))
            em.vsafe()
            t_25 = em.op("vector", lambda v: v.tensor_scalar(
                stat[:, 25:26], stat[:, 30:31], lamneg[:, 0:1], None, ALU.mult, ALU.bypass))
            if st_prev["dT"] is not None:
                em.wait("scalar", st_prev["dT"])
            em.wait("scalar", t_25)
            t_d1 = em.op("scalar", lambda s, X=XEp[0][pm1]: s.activation(
                Dbuf[:, :], X[:, :], AF.Copy, scale=stat[:, 24:25]))

        if nm1 is not None:
            pm1 = nm1 % 2
            em.wait("vector", t_d1)
            t_dp = em.op("vector", lambda v, X=XEp[1][pm1]: v.scalar_tensor_tensor(
                Dbuf[:, :], X[:, :], stat[:, 25:26], Dbuf[:, :], ALU.mult, ALU.add))
            em.wait("sync", t_dp)
            pgm1, im1 = divmod(nm1, 4)
            if pgm1 >= 1:
                if t_pv_of.get(pgm1 - 1) is not None:
                    em.wait("sync", t_pv_of[pgm1 - 1])
                else:
                    cell = {}
                    pv_cells.setdefault(pgm1 - 1, []).append(cell)
                    em.wait_deferred("sync", cell)
            t = em.dma("dsync", lambda s, i_=im1: s.dma_start_transpose(
                DTf[:, :, i_, :], Dbuf[:]))
            t_dT[pgm1] = t
            st_prev["dT"] = t

        em.wait("scalar", t_ng)
        for h in range(2):
            t_e = em.op("scalar", lambda s, P=pbw[h], X=XEp[h][par], b0=ngc + h, a0=smc + h: s.activation(
                X[:, :], P[:, :], AF.Exp, bias=stat[:, b0:b0 + 1], accum_out=stat[:, a0:a0 + 1]))
            t_expw[(n, h)] = t_e
            for c in range(4):
                t_sbank[h * 4 + c] = t_e
        prev_n[0] = n

    def emit_chunk_tail_final():
        nm1 = prev_n[0]
        pm1 = nm1 % 2
        em.wait("vector", t_expw[(nm1, 1)])
        em.op("vector", lambda v, sc=16 + 2 * pm1: v.reciprocal(stat[:, 24:25], stat[:, sc:sc + 1]))
        em.op("vector", lambda v, sc=17 + 2 * pm1: v.reciprocal(stat[:, 30:31], stat[:, sc:sc + 1]))
        em.vsafe()
        t_25 = em.op("vector", lambda v: v.tensor_scalar(
            stat[:, 25:26], stat[:, 30:31], lamneg[:, 0:1], None, ALU.mult, ALU.bypass))
        if st_prev["dT"] is not None:
            em.wait("scalar", st_prev["dT"])
        em.wait("scalar", t_25)
        t_d1 = em.op("scalar", lambda s, X=XEp[0][pm1]: s.activation(
            Dbuf[:, :], X[:, :], AF.Copy, scale=stat[:, 24:25]))
        em.wait("vector", t_d1)
        t_dp = em.op("vector", lambda v, X=XEp[1][pm1]: v.scalar_tensor_tensor(
            Dbuf[:, :], X[:, :], stat[:, 25:26], Dbuf[:, :], ALU.mult, ALU.add))
        em.wait("sync", t_dp)
        pgm1, im1 = divmod(nm1, 4)
        if t_pv_of.get(pgm1 - 1) is not None:
            em.wait("sync", t_pv_of[pgm1 - 1])
        t = em.dma("dsync", lambda s, i_=im1: s.dma_start_transpose(
            DTf[:, :, i_, :], Dbuf[:]))
        t_dT[pgm1] = t
        st_prev["dT"] = t

    def emit_pv_epi(pg):
        Q, p = divmod(pg, 8)
        em.wait("tensor", t_dT[pg])
        em.wait("tensor", t_sbank[0])
        if t_attnT.get(pg - 1) is not None:
            em.wait("scalar", t_attnT.get(pg - 1))
        t_pv_l = None
        for kk in range(16):
            t_pv_l = em.op("tensor", lambda te, k=kk, p_=p: te.matmul(
                pb[0][:, :], vtok[:, k, p_ * 128:(p_ + 1) * 128],
                DTf[:, k, :, :].rearrange("p i t -> p (i t)"), start=(k == 0), stop=(k == 15)))
        t_pv_of[pg] = t_pv_l
        for cell in pv_cells.pop(pg, []):
            cell["tok"] = t_pv_l
        em.wait("scalar", t_pv_l)
        t_ac = em.op("scalar", lambda s: s.copy(Abf[:], pb[0][:, :]))
        t_sbank[0] = t_ac
        em.wait("vector", t_ac)
        if st_prev["s2"] is not None:
            em.wait("vector", st_prev["s2"])
        t_a2 = em.op("vector", lambda v: v.tensor_tensor(A2bf[:], Abf[:], Abf[:], ALU.mult))
        em.wait("tensor", t_a2)
        em.wait("tensor", t_sbank[1])
        t_s2 = em.op("tensor", lambda te: te.matmul(pb[1][0:1, :], ones_l[:], A2bf[:], start=True, stop=True))
        st_prev["s2"] = t_s2
        em.wait("scalar", t_s2)
        em.op("scalar", lambda s: s.activation(grow[0:1, :], pb[1][0:1, :], AF.Ln, scale=1.0 / 128, bias=epsc[0:1, 0:1]))
        t_g = em.op("scalar", lambda s: s.activation(grow[0:1, :], grow[0:1, :], AF.Exp, scale=-0.5, bias=lgc[0:1, 0:1]))
        t_sbank[1] = t_g
        em.wait("vector", t_g)
        t_gbf = em.op("vector", lambda v: v.tensor_copy(growbf[0:1, :], grow[0:1, :]))
        em.wait("tensor", t_gbf)
        em.wait("tensor", t_sbank[2])
        t_gb = em.op("tensor", lambda te: te.matmul(pb[2][:, :], ones_c[:], growbf[0:1, :], start=True, stop=True))
        em.wait("scalar", t_gb)
        t_gc = em.op("scalar", lambda s: s.copy(gBsb[:], pb[2][:, :]))
        t_sbank[2] = t_gc
        em.wait("vector", t_gc)
        if Q >= 1:
            em.wait("vector", t_wo_mm.get(Q - 1))
        t_at = em.op("vector", lambda v, p_=p: v.tensor_tensor(
            attnT[:, p_, :], Abf[:], gBsb[:], ALU.mult))
        t_attnT[pg] = t_at

    t_wo_ld = {}
    t_wo_prevmm = [None]

    def emit_wo(Q):
        em.wait("tensor", t_attnT[Q * 8 + 7])
        t_res = {}
        t_mm = None
        for n in range(2):
            if (Q, n) not in t_wo_ld:
                if t_mm is not None:
                    em.wait("gpsimd", t_mm)
                elif Q > 0:
                    em.wait("gpsimd", t_wo_prevmm[0])
                t_wo_ld[(Q, n)] = em.dma("dgp", lambda g, n_=n: g.dma_start(
                    wo8[:], d_woT[:, n_ * 512:(n_ + 1) * 512].rearrange("(k p) e -> p k e", p=128)))
            em.wait("tensor", t_wo_ld[(Q, n)])
            for m in range(4):
                mn = m * 2 + n if False else n * 4 + m
                tchunk = Q * 4 + m
                bid = 3 + mn % 4
                em.wait("tensor", t_sbank[bid])
                em.wait("tensor", t_res.get(mn - 4))
                for kk in range(8):
                    t_mm = em.op("tensor", lambda te, b=pb[bid], k=kk, m_=m: te.matmul(
                        b[:, :], attnT[:, k, m_ * 128:(m_ + 1) * 128],
                        wo8[:, k, :], start=(k == 0), stop=(k == 7)))
                em.wait("vector", t_mm)
                em.op("vector", lambda v, b=pb[bid], n_=n: v.tensor_tensor(
                    wotmp[:, :], b[:, :], gaB[:, n_ * 512:(n_ + 1) * 512], ALU.mult))
                t_res[mn] = em.op("vector", lambda v, tc=tchunk, n_=n: v.tensor_tensor(
                    x_res[:, tc, n_ * 512:(n_ + 1) * 512], x_res[:, tc, n_ * 512:(n_ + 1) * 512],
                    wotmp[:, :], ALU.add))
                t_sbank[bid] = t_res[mn]
        t_wo_prevmm[0] = t_mm
        if "dbg_attnT" in dbg:
            em.wait("sync", t_attnT[Q * 8 + 7])
            t_dmp = em.dma("dsync", lambda s, Q_=Q: s.dma_start(dbg_outs["dbg_attnT"][Q_], attnT[:]))
            em.wait("vector", t_dmp)
        t_wo_mm[Q] = t_mm

    for pg in range(32):
        emit_chunk_front(pg * 4 + 0)
        emit_chunk_front(pg * 4 + 1)
        if pg > 0:
            emit_pv_epi(pg - 1)
        if pg % 8 == 0 and pg > 0:
            emit_wo(pg // 8 - 1)
        pop_quant(1)
        emit_chunk_front(pg * 4 + 2)
        emit_chunk_front(pg * 4 + 3)
        pop_quant(1)
    emit_chunk_tail_final()
    emit_pv_epi(31)
    emit_wo(3)

    t_x2 = em.now("vector")
    if "dbg_x2" in dbg:
        em.wait("sync", t_x2)
        em.dma("dsync", lambda s: s.dma_start(dbg_outs["dbg_x2"], x_res[:]))

    for cm in (wotmp_cm, wo8_cm, attnT_cm, gBsb_cm, A2bf_cm, Abf_cm, DTf_cm,
               Dbuf_cm, XE11_cm, XE10_cm, XE00_cm, vtok_cm, kT_cm, qT_cm):
        cm.__exit__(None, None, None)

    em.fence()
    if stop_after == 3:
        return _early_finish()

    hiT, hiT_cm = sbuf("hiT", [128, 8, T], BF16)
    rw_sb, rw_sb_cm = sbuf("rw_sb", [128, 8, 8], BF16)
    rjunk, _ = sbuf("rjunk", [128, 8], BF16)
    zb2, zb2_cm = sbuf("zb2", [128, E], F32)
    logit = mg
    t_rw = em.dma("dgp", lambda g: g.dma_start(rw_sb[:], d_rwT.rearrange("(k p) e -> p k e", p=128)))
    t_hic = {}
    bank_ev = {4: None, 5: None, 6: None, 7: None}
    rt_prev = [None]

    def emit_ln2(i):
        em.op("vector", lambda v, i_=i: v.reduce_sum(stat[:, 3:4], x_res[:, i_, :], AX.X))
        em.op("vector", lambda v, i_=i: v.scalar_tensor_tensor(
            geb[:, :], x_res[:, i_, :], 1.0, x_res[:, i_, :], ALU.mult, ALU.mult, accum_out=stat[:, 4:5]))
        em.vsafe()
        em.op("vector", lambda v: v.tensor_scalar(stat[:, 0:1], stat[:, 3:4], 1.0 / E, None, ALU.mult, ALU.bypass))
        em.vsafe()
        em.op("vector", lambda v: v.tensor_tensor(stat[:, 6:7], stat[:, 0:1], stat[:, 0:1], ALU.mult))
        em.op("vector", lambda v: v.tensor_scalar(stat[:, 5:6], stat[:, 4:5], 1.0 / E, None, ALU.mult, ALU.bypass))
        em.vsafe()
        t = em.op("vector", lambda v: v.tensor_scalar(stat[:, 1:2], stat[:, 5:6], stat[:, 6:7], 1e-5, ALU.subtract, ALU.add))
        em.wait("scalar", t)
        em.op("scalar", lambda s: s.activation(stat[:, 2:3], stat[:, 1:2], AF.Ln))
        t = em.op("scalar", lambda s: s.activation(stat[:, 2:3], stat[:, 2:3], AF.Exp, scale=-0.5))
        em.wait("vector", t)
        em.op("vector", lambda v, i_=i: v.tensor_scalar(
            zb2[:, :], x_res[:, i_, :], stat[:, 0:1], stat[:, 2:3], ALU.subtract, ALU.mult))
        em.op("vector", lambda v: v.scalar_tensor_tensor(zb2[:, :], zb2[:, :], 1.0, cb1B[:, :], ALU.mult, ALU.mult))
        em.op("vector", lambda v: v.tensor_tensor(zb2[:, :], zb2[:, :], sbB[:, :], ALU.add))
        em.op("vector", lambda v: v.reduce_sum(stat[:, 3:4], zb2[:, :], AX.X))
        em.op("vector", lambda v: v.scalar_tensor_tensor(
            geb[:, :], zb2[:, :], 1.0, zb2[:, :], ALU.mult, ALU.mult, accum_out=stat[:, 4:5]))
        em.vsafe()
        em.op("vector", lambda v: v.tensor_scalar(stat[:, 0:1], stat[:, 3:4], 1.0 / E, None, ALU.mult, ALU.bypass))
        em.vsafe()
        em.op("vector", lambda v: v.tensor_tensor(stat[:, 6:7], stat[:, 0:1], stat[:, 0:1], ALU.mult))
        em.op("vector", lambda v: v.tensor_scalar(stat[:, 5:6], stat[:, 4:5], 1.0 / E, None, ALU.mult, ALU.bypass))
        em.vsafe()
        t = em.op("vector", lambda v: v.tensor_scalar(stat[:, 1:2], stat[:, 5:6], stat[:, 6:7], 1e-5, ALU.subtract, ALU.add))
        em.wait("scalar", t)
        em.op("scalar", lambda s: s.activation(stat[:, 2:3], stat[:, 1:2], AF.Ln))
        t = em.op("scalar", lambda s: s.activation(stat[:, 2:3], stat[:, 2:3], AF.Exp, scale=-0.5))
        em.wait("vector", t)
        t_z2 = em.op("vector", lambda v: v.tensor_scalar(
            zb2[:, :], zb2[:, :], stat[:, 0:1], stat[:, 2:3], ALU.subtract, ALU.mult))
        em.wait("tensor", t_z2)
        t_ev = None
        for j in range(8):
            bk = 4 + (j % 4)
            em.wait("tensor", bank_ev[bk])
            t_t = em.op("tensor", lambda te, j_=j, b=pb[bk]: te.transpose(
                b[:, 0:128], zb2[:, j_ * 128:(j_ + 1) * 128], idf[:]))
            em.wait("vector", t_t)
            t_ev = em.op("vector", lambda v, i_=i, j_=j, b=pb[bk]: v.tensor_scalar(
                hiT[:, j_, i_ * 128:(i_ + 1) * 128], b[:, 0:128],
                gamc[:, j_:j_ + 1], betc[:, j_:j_ + 1], ALU.mult, ALU.add))
            bank_ev[bk] = t_ev
        t_hic[i] = t_ev

    def emit_router(i):
        em.wait("tensor", t_rw)
        em.wait("tensor", bank_ev[7])
        em.wait("tensor", rt_prev[0])
        for kk in range(8):
            t_mm = em.op("tensor", lambda te, k=kk, i_=i: te.matmul(
                pb[7][:, 0:8], hiT[:, k, i_ * 128:(i_ + 1) * 128], rw_sb[:, k, :],
                start=(k == 0), stop=(k == 7)))
        em.wait("vector", t_mm)
        t_lc = em.op("vector", lambda v: v.tensor_copy(logit[:], pb[7][:, 0:8]))
        bank_ev[7] = t_lc
        rt_prev[0] = t_lc
        em.vsafe()
        em.op("vector", lambda v: v.reduce_max(stat[:, 0:1], logit[:], AX.X))
        em.vsafe()
        t_nm = em.op("vector", lambda v: v.tensor_scalar(stat[:, 1:2], stat[:, 0:1], -1.0, None, ALU.mult, ALU.bypass))
        em.wait("scalar", t_nm)
        t_e8 = em.op("scalar", lambda s: s.activation(rjunk[:, :], logit[:], AF.Exp,
                                                      bias=stat[:, 1:2], accum_out=stat[:, 2:3]))
        em.wait("vector", t_e8)
        em.op("vector", lambda v: v.reciprocal(stat[:, 3:4], stat[:, 2:3]))
        em.vsafe()
        em.op("vector", lambda v, i_=i: v.tensor_scalar(
            gates[:, i_, :], logit[:], stat[:, 0:1], stat[:, 3:4], ALU.is_equal, ALU.mult))

    QB["prefetch"] = True
    w1q, w1q_cm = sbuf("w1q", [128, 8, FF], BF16)
    w2q, w2q_cm = sbuf("w2q", [128, 16, E], BF16)
    h1T, h1T_cm = sbuf("h1T", [128, 16, 512], BF16)
    racc, racc_cm = sbuf("racc", [128, 16, E], BF16)

    for i in range(4):
        emit_ln2(i)

    t_rac = {}
    t_lastmm1 = None
    t_lastmm2 = None
    for mat in range(9):
        if 60 <= stop_after < 69 and mat > stop_after - 60:
            break
        while len(quant_jobs) + len(_primed) > 24 * (8 - mat):
            pop_quant(1)
        em.wait("sync", t_qst1.get(mat))
        em.wait("sync", t_qst2.get(mat))
        if t_lastmm1 is not None:
            em.wait("sync", t_lastmm1)
        t_w1r = em.dma("dsync", lambda s, m=mat: s.dma_start_transpose(w1q[:], d_q1[m]))
        if t_lastmm2 is not None:
            em.wait("sync", t_lastmm2)
        t_w2r = em.dma("dsync", lambda s, m=mat: s.dma_start_transpose(w2q[:], d_q2[m]))
        if "dbg_w1q" in dbg and mat == 1:
            em.wait("sync", t_w2r)
            em.dma("dsync", lambda s: s.dma_start(dbg_outs["dbg_w1q"], w1q[:]))
            em.dma("dsync", lambda s: s.dma_start(dbg_outs["dbg_w2q"], w2q[:]))
        em.wait("tensor", t_w1r)
        t_sil = {}
        t_mm2_last_q = None
        for Q in range(4):
            if t_mm2_last_q is not None:
                em.wait("scalar", t_mm2_last_q)
            if mat == 0:
                for tc in range(4 * Q, 4 * Q + 4):
                    em.wait("tensor", t_hic[tc])
            for m in range(16):
                bank = pb[m % 2]
                em.wait("tensor", t_sil.get(m - 2))
                t_mm = None
                for kk in range(8):
                    t_mm = em.op("tensor", lambda te, b=bank, m_=m, k=kk, Q_=Q: te.matmul(
                        b[:, :], w1q[:, k, m_ * 128:(m_ + 1) * 128],
                        hiT[:, k, Q_ * 512:(Q_ + 1) * 512], start=(k == 0), stop=(k == 7)))
                em.wait("scalar", t_mm)
                t_sil[m] = em.op("scalar", lambda s, b=bank, m_=m: s.activation(
                    h1T[:, m_, :], b[:, :], AF.Silu))
                if m % 4 == 1:
                    pop_quant(1)
                if mat == 0 and Q < 3 and m % 4 == 3:
                    emit_ln2(4 * (Q + 1) + m // 4)
                    pop_quant(1)
            t_lastmm1 = t_mm
            em.wait("tensor", t_sil[15])
            em.wait("tensor", t_w2r)
            for mt in range(4):
                tchunk = Q * 4 + mt
                for n in range(2):
                    bank = pb[2 + (mt * 2 + n) % 2]
                    em.wait("tensor", t_rac.get((tchunk, n)))
                    t_mm2 = None
                    for kk in range(16):
                        t_mm2 = em.op("tensor", lambda te, b=bank, k=kk, mt_=mt, n_=n: te.matmul(
                            b[:, :], h1T[:, k, mt_ * 128:(mt_ + 1) * 128],
                            w2q[:, k, n_ * 512:(n_ + 1) * 512], start=(k == 0), stop=(k == 15)))
                    t_mm2_last_q = t_mm2
                    if mat == 0:
                        em.wait("scalar", t_mm2)
                        t_rac[(tchunk, n)] = em.op("scalar", lambda s, b=bank, tc=tchunk, n_=n: s.copy(
                            racc[:, tc, n_ * 512:(n_ + 1) * 512], b[:, :]))
                    else:
                        em.wait("vector", t_mm2)
                        em.wait("vector", t_rac[(tchunk, n)])
                        t_rac[(tchunk, n)] = em.op("vector", lambda v, b=bank, tc=tchunk, n_=n, m_=mat: v.scalar_tensor_tensor(
                            racc[:, tc, n_ * 512:(n_ + 1) * 512], b[:, :], gates[:, tc, m_ - 1:m_],
                            racc[:, tc, n_ * 512:(n_ + 1) * 512], ALU.mult, ALU.add))
            t_lastmm2 = t_mm2
            if mat == 0:
                for tc in range(4 * Q, 4 * Q + 4):
                    emit_router(tc)
                    pop_quant(1)
        if mat == 0:
            if "dbg_hiT" in dbg:
                em.wait("sync", t_hic[15])
                em.dma("dsync", lambda s: s.dma_start(dbg_outs["dbg_hiT"], hiT[:]))
            if "dbg_gate" in dbg:
                em.wait("sync", em.now("vector"))
                em.dma("dsync", lambda s: s.dma_start(dbg_outs["dbg_gate"], gates[:]))

    em.fence()

    t_ost = {}
    for i in range(16):
        if i >= 1:
            em.wait("vector", t_ost[i - 1])
        em.op("vector", lambda v, i_=i: v.scalar_tensor_tensor(
            racc[:, i_, :], racc[:, i_, :], 1.0, gbB[:, :], ALU.mult, ALU.mult))
        t_fin = em.op("vector", lambda v, i_=i: v.tensor_tensor(
            zb2[:, :], x_res[:, i_, :], racc[:, i_, :], ALU.add))
        em.wait("sync", t_fin)
        t_ost[i] = em.dma("dsync", lambda s, i_=i: s.dma_start(
            d_out[i_ * 128:(i_ + 1) * 128, :], zb2[:, :]))

    em.final_wait()

    with nc.Block() as block:
        em.replay(block)
    ctx.close()
    return nc


def _prep_inputs(inputs, core):
    f32c = lambda a: np.ascontiguousarray(a, dtype=np.float32)
    Cq, Sq, Ck, Sk = _rope_tables()
    return {
        "x": f32c(inputs["L0_attn_x"][core]),
        "wqT": f32c(np.asarray(inputs["L0_attn_Wq"]).T[:, _PERM_E]),
        "wkT": f32c(np.asarray(inputs["L0_attn_Wk"]).T[:, _PERM_E]),
        "wvT": f32c(np.asarray(inputs["L0_attn_Wv"]).T),
        "woT": f32c(np.asarray(inputs["L0_attn_Wo"]).T),
        "wmod": f32c(inputs["L0_wmod"]),
        "c8": f32c(np.asarray(inputs["L0_c"])[core].reshape(8, 128).T),
        "lamv": f32c(np.concatenate([np.asarray(inputs["L0_attn_lambda_q1"]), np.asarray(inputs["L0_attn_lambda_k1"]),
                                     np.asarray(inputs["L0_attn_lambda_q2"]), np.asarray(inputs["L0_attn_lambda_k2"])]).reshape(1, 256)),
        "rwT": f32c(np.asarray(inputs["L0_ffn_router_w"]).T),
        "gam": f32c(np.asarray(inputs["L0_ffn_ln_gamma"]).reshape(8, 128).T),
        "bet": f32c(np.asarray(inputs["L0_ffn_ln_beta"]).reshape(8, 128).T),
        "sw1": f32c(inputs["L0_ffn_shared_w1"]),
        "sw2": f32c(inputs["L0_ffn_shared_w2"]),
        "ew1": f32c(inputs["L0_ffn_expert_w1"]),
        "ew2": f32c(inputs["L0_ffn_expert_w2"]),
        "idb": np.eye(128, dtype=ml_dtypes.bfloat16),
        "idf": np.eye(128, dtype=np.float32),
        "Cq": Cq, "Sq": Sq, "Ck": Ck, "Sk": Sk,
    }


def kernel(**inputs):
    nc = build_program()
    in_maps = [_prep_inputs(inputs, core) for core in range(8)]
    res = run_bass_kernel_spmd(nc, in_maps, core_ids=list(range(8)))
    return np.stack([np.asarray(r["out"]).astype(np.float32) for r in res.results], axis=0)


# version 09 (final)
# speedup vs baseline: 1.0255x; 1.0255x over previous
import math
from contextlib import ExitStack

import numpy as np
import ml_dtypes

import concourse.bass as bass
import concourse.mybir as mybir
from concourse.bass_utils import run_bass_kernel_spmd

F32 = mybir.dt.float32
BF16 = mybir.dt.bfloat16
F32R = mybir.dt.float32r
AF = mybir.ActivationFunctionType
ALU = mybir.AluOpType
AX = mybir.AxisListType

T = 2048
E = 1024
FF = 2048
NEXP = 8
LAMBDA_INIT = 0.8 - 0.6 * math.exp(-0.3)
LOG_1ML = math.log(1.0 - LAMBDA_INIT)


def _rope_tables():
    D = 64
    inv = 1.0 / 10000 ** (np.arange(0, D, 2) / D)
    pos = np.arange(T, dtype=np.float64)
    fr = pos[:, None] * inv[None, :]
    sv = (np.arange(0, D, 2) + 0.4 * D) / (1.4 * D)
    power = (pos - T // 2) / 512.0
    scl = sv[None, :] ** power[:, None]
    i32 = np.arange(32)
    se = scl[:, (2 * i32) % 32]
    so = scl[:, (2 * i32 + 1) % 32]
    cos, sin = np.cos(fr), np.sin(fr)
    Cq = np.concatenate([(cos * se).T, (cos * so).T] * 2, 0)
    Sq = np.concatenate([(-sin * se).T, (sin * so).T] * 2, 0)
    sei, soi = 1.0 / se, 1.0 / so
    Ck = np.concatenate([(cos * sei).T, (cos * soi).T] * 2, 0)
    Sk = np.concatenate([(-sin * sei).T, (sin * soi).T] * 2, 0)
    s = 64.0 ** -0.5
    bfc = lambda a: np.ascontiguousarray(a.astype(np.float32), dtype=ml_dtypes.bfloat16)
    return bfc(Cq * s), bfc(Sq * s), bfc(Ck), bfc(Sk)


_PERM64 = np.concatenate([np.arange(0, 64, 2), np.arange(1, 64, 2)])
_PERM_E = np.concatenate([h * 64 + _PERM64 for h in range(16)])


class Emit:
    ENGS = ("tensor", "vector", "scalar", "sync", "gpsimd")
    _ESEM = {"tensor": "pe", "vector": "ve", "scalar": "se", "gpsimd": "gp",
             "dsync": "dS", "dgp": "dG"}

    def __init__(self, nc):
        self.nc = nc
        self.cmds = {e: [] for e in self.ENGS}
        self.cnt = {e: 0 for e in self.ENGS}
        self.dma_cnt = {"dsync": 0, "dgp": 0}
        self.sems = {}
        self.watermark = {}

    def make_sems(self, ctx):
        for name in ("pe", "ve", "se", "gp", "dS", "dG"):
            self.sems[name] = ctx.enter_context(self.nc.semaphore(name))

    def wait(self, waiter, token):
        if token is None:
            return
        prod, count = token
        if count <= 0:
            return
        sem_name = self._ESEM[prod]
        val = count * 16 if prod in ("dsync", "dgp") else count
        key = (waiter, sem_name)
        if self.watermark.get(key, 0) >= val:
            return
        self.watermark[key] = val
        sem = self.sems[sem_name]
        self.cmds[waiter].append(lambda eng, s=sem, v=val: eng.wait_ge(s, v))

    def vsafe(self):
        self.wait("vector", self.now("vector"))

    def wait_deferred(self, waiter, cell):
        def w(eng, c=cell):
            prod, count = c["tok"]
            val = count * 16 if prod in ("dsync", "dgp") else count
            return eng.wait_ge(self.sems[self._ESEM[prod]], val)
        self.cmds[waiter].append(w)

    def op(self, eng_name, fn):
        sem = self.sems[self._ESEM[eng_name]]
        self.cmds[eng_name].append(lambda eng, f=fn, s=sem: f(eng).then_inc(s, 1))
        self.cnt[eng_name] += 1
        return (eng_name, self.cnt[eng_name])

    def dma(self, queue, fn):
        eng = {"dsync": "sync", "dgp": "gpsimd"}[queue]
        sem = self.sems[self._ESEM[queue]]
        self.cmds[eng].append(lambda e, f=fn, s=sem: f(e).then_inc(s, 16))
        self.dma_cnt[queue] += 1
        return (queue, self.dma_cnt[queue])

    def now(self, eng_name):
        if eng_name in ("dsync", "dgp"):
            return (eng_name, self.dma_cnt[eng_name])
        return (eng_name, self.cnt[eng_name])

    def fence(self):
        toks = [self.now(e) for e in ("tensor", "vector", "scalar", "gpsimd")]
        toks += [self.now("dsync"), self.now("dgp")]
        for w in ("tensor", "vector", "scalar", "sync", "gpsimd"):
            for t in toks:
                if (w, t[0]) in (("sync", "dsync"), ("gpsimd", "dgp")):
                    continue
                if t[0] == w:
                    continue
                self.wait(w, t)

    def final_wait(self):
        self.wait("sync", ("dsync", self.dma_cnt["dsync"]))
        self.wait("sync", ("dgp", self.dma_cnt["dgp"]))

    def replay(self, block):
        for name in self.ENGS:
            sec = getattr(block, name)

            def runner(eng, _cmds=self.cmds[name]):
                for c in _cmds:
                    c(eng)
            sec(runner)


_QMAX = [10**9]


def build_program(dbg=(), stop_after=99):
    nc = bass.Bass("TRN2", target_bir_lowering=False)

    def din(nm, shape, dt=F32):
        return nc.dram_tensor(nm, shape, dt, kind="ExternalInput").ap()

    d_x = din("x", [T, E])
    d_wqT = din("wqT", [E, E]); d_wkT = din("wkT", [E, E])
    d_wvT = din("wvT", [E, E]); d_woT = din("woT", [E, E])
    d_wmod = din("wmod", [E, 6 * E])
    d_c8 = din("c8", [128, 8])
    d_lam = din("lamv", [1, 256])
    d_rwT = din("rwT", [E, 8])
    d_gam = din("gam", [128, 8]); d_bet = din("bet", [128, 8])
    d_sw1 = din("sw1", [FF, E]); d_sw2 = din("sw2", [E, FF])
    d_ew1 = din("ew1", [NEXP, FF, E]); d_ew2 = din("ew2", [NEXP, E, FF])
    d_idb = din("idb", [128, 128], BF16)
    d_idf = din("idf", [128, 128], F32)
    d_Cq = din("Cq", [128, T], BF16); d_Sq = din("Sq", [128, T], BF16)
    d_Ck = din("Ck", [128, T], BF16); d_Sk = din("Sk", [128, T], BF16)
    d_out = nc.dram_tensor("out", [T, E], F32, kind="ExternalOutput").ap()
    d_q1 = nc.dram_tensor("q1scr", [9, FF, E], BF16).ap()
    d_q2 = nc.dram_tensor("q2scr", [9, E, FF], BF16).ap()

    dbg_specs = {
        "dbg_mod": ([1, 6 * E], F32), "dbg_haT": ([128, 8, T], BF16),
        "dbg_qT": ([128, 8, T], BF16), "dbg_kT": ([128, 8, T], BF16),
        "dbg_v": ([128, 16, E], BF16), "dbg_attnT": ([4, 128, 8, 512], BF16),
        "dbg_x2": ([128, 16, E], BF16), "dbg_hiT": ([128, 8, T], BF16),
        "dbg_gate": ([128, 16, 8], F32), "dbg_w1q": ([128, 8, FF], BF16),
        "dbg_w2q": ([128, 16, E], BF16),
    }
    dbg_outs = {nm: nc.dram_tensor(nm, *dbg_specs[nm], kind="ExternalOutput").ap()
                for nm in dbg}

    ctx = ExitStack()
    em = Emit(nc)
    em.make_sems(ctx)

    def sbuf(nm, shape, dt=F32):
        cm = nc.sbuf_tensor(nm, shape, dt)
        return cm.__enter__(), cm

    x_res, _ = sbuf("x_res", [128, 16, E], BF16)
    idf, _ = sbuf("idf_sb", [128, 128], F32)
    ones_c, _ = sbuf("ones_c", [1, 128], BF16)
    ones_l, _ = sbuf("ones_l", [128, 1], BF16)
    grow, _ = sbuf("grow", [1, 512], F32)
    epsc, _ = sbuf("epsc", [1, 1], F32)
    lgc, _ = sbuf("lgc", [1, 1], F32)
    growbf, _ = sbuf("growbf", [1, 512], BF16)
    gaB, _ = sbuf("gaB", [128, E], BF16)
    gbB, _ = sbuf("gbB", [128, E], BF16)
    cb1B, _ = sbuf("cb1B", [128, E], BF16)
    sbB, _ = sbuf("sbB", [128, E], BF16)
    ca1c, _ = sbuf("ca1c", [128, 8], F32)
    sac, _ = sbuf("sac", [128, 8], F32)
    gamc, _ = sbuf("gamc", [128, 8], F32)
    betc, _ = sbuf("betc", [128, 8], F32)
    c8, _ = sbuf("c8_sb", [128, 8], F32)
    lamrows, _ = sbuf("lamrows", [1, 256], F32)
    lamneg, _ = sbuf("lamneg", [128, 1], F32)
    gates, _ = sbuf("gates", [128, 16, 8], F32)
    stat, _ = sbuf("stat", [128, 32], F32)
    mg, _ = sbuf("mgcols", [128, 8], F32)
    pbw = [nc.psum_tensor(f"bankw{j}", [128, 2048], F32).__enter__() for j in range(2)]
    pb = [pbw[j // 4][:, (j % 4) * 512:(j % 4 + 1) * 512] for j in range(8)]


    def _early_finish():
        ostE, _ = sbuf("ostE", [128, 2, E], F32)
        em.fence()
        t_prev = {}
        for i2 in range(16):
            sl2 = i2 % 2
            if i2 >= 2:
                em.wait("vector", t_prev[i2 - 2])
            t_f = em.op("vector", lambda v, i_=i2, s_=sl2: v.tensor_copy(ostE[:, s_, :], x_res[:, i_, :]))
            em.wait("sync", t_f)
            t_prev[i2] = em.dma("dsync", lambda s, i_=i2, s_=sl2: s.dma_start(
                d_out[i_ * 128:(i_ + 1) * 128, :], ostE[:, s_, :]))
        em.final_wait()
        with nc.Block() as blk:
            em.replay(blk)
        ctx.close()
        return nc

    modrow, modrow_cm = sbuf("modrow", [1, 6 * E], F32)
    rwk, rwk_cm = sbuf("rwk", [1, 4 * E], F32)
    rbf, rbf_cm = sbuf("rbf", [1, 4 * E], BF16)
    for dst, src in [(idf, d_idf), (c8, d_c8), (lamrows, d_lam),
                     (gamc, d_gam), (betc, d_bet)]:
        t_ld0 = em.dma("dsync", lambda s, d=dst, a=src: s.dma_start(d[:], a))
    if stop_after == -1:
        return _early_finish()
    em.wait("vector", t_ld0)
    em.op("vector", lambda v: v.memset(ones_c[:], 1.0))
    em.op("vector", lambda v: v.memset(ones_l[:], 1.0))
    em.op("vector", lambda v: v.memset(epsc[:], 1e-5))
    em.op("vector", lambda v: v.memset(lgc[:], LOG_1ML))

    em.op("vector", lambda v: v.tensor_tensor(rwk[0:1, 0:64], lamrows[0:1, 0:64], lamrows[0:1, 64:128], ALU.mult))
    em.op("vector", lambda v: v.tensor_tensor(rwk[0:1, 64:128], lamrows[0:1, 128:192], lamrows[0:1, 192:256], ALU.mult))
    em.vsafe()
    em.op("vector", lambda v: v.reduce_sum(rwk[0:1, 128:129], rwk[0:1, 0:64], AX.X))
    t = em.op("vector", lambda v: v.reduce_sum(rwk[0:1, 129:130], rwk[0:1, 64:128], AX.X))
    em.wait("scalar", t)
    t = em.op("scalar", lambda s: s.activation(rwk[0:1, 130:132], rwk[0:1, 128:130], AF.Exp))
    em.wait("vector", t)
    em.op("vector", lambda v: v.tensor_tensor(rwk[0:1, 132:133], rwk[0:1, 131:132], rwk[0:1, 130:131], ALU.subtract))
    em.vsafe()
    em.op("vector", lambda v: v.tensor_scalar(rwk[0:1, 132:133], rwk[0:1, 132:133], LAMBDA_INIT, None, ALU.subtract, ALU.bypass))
    em.vsafe()
    t = em.op("vector", lambda v: v.tensor_copy(rbf[0:1, 0:1], rwk[0:1, 132:133]))
    em.wait("tensor", t)
    t = em.op("tensor", lambda te: te.matmul(pb[7][:, 0:1], ones_c[:], rbf[0:1, 0:1], start=True, stop=True))
    em.wait("vector", t)
    t_lamneg = em.op("vector", lambda v: v.tensor_copy(lamneg[:], pb[7][:, 0:1]))
    if stop_after == 10:
        return _early_finish()

    scf, scf_cm = sbuf("scf", [128, 8], F32)
    schi, schi_cm = sbuf("schi", [128, 8], BF16)
    sclo, sclo_cm = sbuf("sclo", [128, 8], BF16)
    em.wait("scalar", em.now("dsync"))
    t_sf = em.op("scalar", lambda s: s.activation(scf[:], c8[:], AF.Silu))
    em.wait("vector", t_sf)
    em.op("vector", lambda v: v.tensor_copy(schi[:], scf[:]))
    em.vsafe()
    t_sc = em.op("vector", lambda v: v.tensor_tensor(sclo[:], scf[:], schi[:], ALU.subtract))

    wmf, wmf_cm = sbuf("wmf", [128, 6 * E], F32)
    wmhi, wmhi_cm = sbuf("wmhi", [128, 6 * E], BF16)
    wmlo, wmlo_cm = sbuf("wmlo", [128, 6 * E], BF16)
    t_modz = em.op("vector", lambda v: v.memset(modrow[:], 0.0))
    t_ev = {n: t_modz for n in range(12)}
    t_cp_prev = None
    t_mm_last_prev = None
    em.wait("tensor", t_sc)
    for kk in range(8):
        if t_cp_prev is not None:
            em.wait("sync", t_cp_prev)
        t_ld = em.dma("dsync", lambda s, k=kk: s.dma_start(wmf[:], d_wmod[k * 128:(k + 1) * 128, :]))
        em.wait("vector", t_ld)
        if t_mm_last_prev is not None:
            em.wait("vector", t_mm_last_prev)
        em.op("vector", lambda v: v.tensor_copy(wmhi[:], wmf[:]))
        t_cp = em.op("vector", lambda v: v.tensor_tensor(wmlo[:], wmf[:], wmhi[:], ALU.subtract))
        t_cp_prev = t_cp
        em.wait("tensor", t_cp)
        for n in range(12):
            bank = pb[n % 4]
            em.wait("tensor", t_ev[n])
            em.op("tensor", lambda te, b=bank, k=kk, n_=n: te.matmul(
                b[0:1, :], schi[:, k:k + 1], wmhi[:, n_ * 512:(n_ + 1) * 512], start=True, stop=False))
            em.op("tensor", lambda te, b=bank, k=kk, n_=n: te.matmul(
                b[0:1, :], schi[:, k:k + 1], wmlo[:, n_ * 512:(n_ + 1) * 512], start=False, stop=False))
            t_mm = em.op("tensor", lambda te, b=bank, k=kk, n_=n: te.matmul(
                b[0:1, :], sclo[:, k:k + 1], wmhi[:, n_ * 512:(n_ + 1) * 512], start=False, stop=True))
            em.wait("vector", t_mm)
            t_ev[n] = em.op("vector", lambda v, b=bank, n_=n: v.tensor_tensor(
                modrow[0:1, n_ * 512:(n_ + 1) * 512], modrow[0:1, n_ * 512:(n_ + 1) * 512],
                b[0:1, :], ALU.add))
        t_mm_last_prev = t_mm
    t_mod = max(t_ev.values(), key=lambda x: x[1])
    if stop_after == 11:
        return _early_finish()
    wmlo_cm.__exit__(None, None, None)
    wmhi_cm.__exit__(None, None, None)
    wmf_cm.__exit__(None, None, None)
    sclo_cm.__exit__(None, None, None)
    schi_cm.__exit__(None, None, None)
    scf_cm.__exit__(None, None, None)

    if "dbg_mod" in dbg:
        em.wait("sync", t_mod)
        em.dma("dsync", lambda s: s.dma_start(dbg_outs["dbg_mod"], modrow[:]))

    SA, CA, GA, SB, CB, GB = range(6)

    def row(i, lo=0, hi=E):
        return modrow[0:1, i * E + lo:i * E + hi]

    for gi, g_idx in enumerate([GA, GB]):
        hi_ = rwk[0:1, 0:E]; lo_ = rwk[0:1, E:2 * E]
        d_ = rwk[0:1, 2 * E:3 * E]; m_ = rwk[0:1, 3 * E:4 * E]
        em.op("vector", lambda v, g=g_idx, o=hi_: v.tensor_scalar(o, row(g), 0.25, None, ALU.max, ALU.bypass))
        em.op("vector", lambda v, g=g_idx, o=lo_: v.tensor_scalar(o, row(g), -0.25, None, ALU.min, ALU.bypass))
        em.op("vector", lambda v, g=g_idx, o=m_: v.tensor_scalar(o, row(g), 0.0, 1.0, ALU.is_ge, ALU.mult))
        em.vsafe()
        em.op("vector", lambda v, a=hi_, b=lo_, o=d_: v.tensor_tensor(o, a, b, ALU.subtract))
        em.vsafe()
        em.op("vector", lambda v, a=d_, b=m_, o=d_: v.tensor_tensor(o, a, b, ALU.mult))
        em.vsafe()
        em.op("vector", lambda v, a=lo_, b=d_, gi_=gi: v.tensor_tensor(
            rbf[0:1, gi_ * E:(gi_ + 1) * E], a, b, ALU.add))
    em.op("vector", lambda v: v.tensor_scalar(rbf[0:1, 2 * E:3 * E], row(CB), 1.0, None, ALU.add, ALU.bypass))
    t_rows = em.op("vector", lambda v: v.tensor_copy(rbf[0:1, 3 * E:4 * E], row(SB)))

    em.wait("tensor", t_rows)
    for dst, off in [(gaB, 0), (gbB, E), (cb1B, 2 * E), (sbB, 3 * E)]:
        for h in range(2):
            b = pb[4 + h]
            tm = em.op("tensor", lambda te, b_=b, o=off, h_=h: te.matmul(
                b_[:, :], ones_c[:], rbf[0:1, o + h_ * 512:o + (h_ + 1) * 512], start=True, stop=True))
            em.wait("scalar", tm)
            tc = em.op("scalar", lambda s, b_=b, d=dst, h_=h: s.copy(d[:, h_ * 512:(h_ + 1) * 512], b_[:, :]))
            em.wait("tensor", tc)

    if stop_after == 13:
        return _early_finish()
    t_ca = em.op("vector", lambda v: v.tensor_scalar(rwk[0:1, 0:E], row(CA), 1.0, None, ALU.add, ALU.bypass))
    em.wait("tensor", t_ca)
    em.wait("tensor", t_lamneg)
    t_col_ev = None
    for dst, src_fn in [(ca1c, lambda j: rwk[0:1, j * 128:(j + 1) * 128]),
                        (sac, lambda j: row(SA, j * 128, (j + 1) * 128))]:
        for j in range(8):
            b = pb[6 + (j % 2)]
            if t_col_ev is not None:
                em.wait("tensor", t_col_ev)
            tm = em.op("tensor", lambda te, b_=b, s_=src_fn(j): te.transpose(
                b_[:, 0:1], s_, idf[0:1, 0:1]))
            em.wait("vector", tm)
            t_col_ev = em.op("vector", lambda v, b_=b, d=dst, j_=j: v.tensor_copy(d[:, j_:j_ + 1], b_[:, 0:1]))

    rbf_cm.__exit__(None, None, None)
    rwk_cm.__exit__(None, None, None)
    modrow_cm.__exit__(None, None, None)

    em.fence()
    if stop_after == 0:
        return _early_finish()

    wst, wst_cm = sbuf("wst", [128, 2, E], F32)
    geb, geb_cm = sbuf("geb", [128, E], BF16)
    tqb, tqb_cm = sbuf("tqb", [128, 2, E], BF16)
    qT, qT_cm = sbuf("qT", [128, 8, T], BF16)
    kT, kT_cm = sbuf("kT", [128, 8, T], BF16)
    vtok, vtok_cm = sbuf("vtok", [128, 16, E], BF16)
    haT, haT_cm = sbuf("haT", [128, 8, T], BF16)
    xstg, xstg_cm = sbuf("xstg", [128, 2, E], F32)
    zbuf, zbuf_cm = sbuf("zbuf", [128, 2, E], F32)
    junk1, junk1_cm = sbuf("junk1", [128, E], BF16)

    t_xuse = {}
    t_trans = {}
    t_evac = {}
    for i in range(16):
        sl = i % 2
        if i >= 2:
            em.wait("sync", t_xuse[i - 2])
        t_xld = em.dma("dsync", lambda s, i_=i, sl_=sl: s.dma_start(
            xstg[:, sl_, :], d_x[i_ * 128:(i_ + 1) * 128, :]))
        em.wait("vector", t_xld)
        em.op("vector", lambda v, sl_=sl: v.reduce_sum(stat[:, 3:4], xstg[:, sl_, :], AX.X))
        em.op("vector", lambda v, sl_=sl: v.scalar_tensor_tensor(
            junk1[:, :], xstg[:, sl_, :], 1.0, xstg[:, sl_, :], ALU.mult, ALU.mult, accum_out=stat[:, 4:5]))
        em.vsafe()
        em.op("vector", lambda v: v.tensor_scalar(stat[:, 0:1], stat[:, 3:4], 1.0 / E, None, ALU.mult, ALU.bypass))
        em.vsafe()
        em.op("vector", lambda v: v.tensor_tensor(stat[:, 6:7], stat[:, 0:1], stat[:, 0:1], ALU.mult))
        em.op("vector", lambda v: v.tensor_scalar(stat[:, 5:6], stat[:, 4:5], 1.0 / E, None, ALU.mult, ALU.bypass))
        em.vsafe()
        t = em.op("vector", lambda v: v.tensor_scalar(stat[:, 1:2], stat[:, 5:6], stat[:, 6:7], 1e-5, ALU.subtract, ALU.add))
        em.wait("scalar", t)
        em.op("scalar", lambda s: s.activation(stat[:, 2:3], stat[:, 1:2], AF.Ln))
        t = em.op("scalar", lambda s: s.activation(stat[:, 2:3], stat[:, 2:3], AF.Exp, scale=-0.5))
        em.wait("vector", t)
        if i >= 2:
            em.wait("vector", t_trans[(i - 2, 7)])
        t_z = em.op("vector", lambda v, sl_=sl: v.tensor_scalar(
            zbuf[:, sl_, :], xstg[:, sl_, :], stat[:, 0:1], stat[:, 2:3], ALU.subtract, ALU.mult))
        t_xuse[i] = em.op("vector", lambda v, i_=i, sl_=sl: v.tensor_copy(x_res[:, i_, :], xstg[:, sl_, :]))
        em.wait("tensor", t_z)
        for j in range(8):
            if i:
                em.wait("tensor", t_evac[(i - 1, j)])
            t_trans[(i, j)] = em.op("tensor", lambda te, sl_=sl, j_=j: te.transpose(
                pb[j_][:, 0:128], zbuf[:, sl_, j_ * 128:(j_ + 1) * 128], idf[:]))
        for j in range(8):
            em.wait("vector", t_trans[(i, j)])
            t_evac[(i, j)] = em.op("vector", lambda v, i_=i, j_=j: v.tensor_scalar(
                haT[:, j_, i_ * 128:(i_ + 1) * 128], pb[j_][:, 0:128],
                ca1c[:, j_:j_ + 1], sac[:, j_:j_ + 1], ALU.mult, ALU.add))
    t_haT = em.now("vector")
    junk1_cm.__exit__(None, None, None)
    zbuf_cm.__exit__(None, None, None)
    xstg_cm.__exit__(None, None, None)

    if "dbg_haT" in dbg:
        em.wait("sync", t_haT)
        em.dma("dsync", lambda s: s.dma_start(dbg_outs["dbg_haT"], haT[:]))

    em.fence()
    if stop_after == 1:
        return _early_finish()

    wbuf, wbuf_cm = sbuf("wbuf", [128, 8, 512], BF16)
    ctab, ctab_cm = sbuf("ctab", [128, T], BF16)
    stab, stab_cm = sbuf("stab", [128, T], BF16)
    rt1, rt1_cm = sbuf("rt1", [128, 512], F32)
    rt2, rt2_cm = sbuf("rt2", [128, 512], F32)

    def proj_qk(dst, d_w, d_C, d_S):
        em.wait("sync", em.now("vector"))
        t_c = em.dma("dsync", lambda s, a=d_C: s.dma_start(ctab[:], a))
        t_s = em.dma("dsync", lambda s, a=d_S: s.dma_start(stab[:], a))
        em.wait("vector", t_s)
        t_rope = {}
        t_w = None
        for mn in range(32):
            m, n = divmod(mn, 4)
            if m % 4 == 0 and n == 0:
                em.wait("gpsimd", em.now("tensor"))
                t_w = em.dma("dgp", lambda g, w=d_w, m_=m: g.dma_start(
                    wbuf[:], w[:, m_ * 128:m_ * 128 + 512].rearrange("(k p) e -> p k e", p=128)))
                em.wait("tensor", t_w)
            bank = pb[mn % 4]
            em.wait("tensor", t_rope.get(mn - 4))
            for kk in range(8):
                t_mm = em.op("tensor", lambda te, b=bank, m_=m, n_=n, k=kk: te.matmul(
                    b[:, :], wbuf[:, k, (m_ % 4) * 128:(m_ % 4 + 1) * 128],
                    haT[:, k, n_ * 512:(n_ + 1) * 512], start=(k == 0), stop=(k == 7)))
            em.wait("vector", t_mm)
            ts_ = slice(n * 512, (n + 1) * 512)
            em.op("vector", lambda v, b=bank, t_=ts_: v.tensor_tensor(rt1[:, :], b[:, :], ctab[:, t_], ALU.mult))
            for l in (0, 64):
                em.op("vector", lambda v, b=bank, t_=ts_, l_=l: v.tensor_tensor(
                    rt2[l_:l_ + 32, :], b[l_ + 32:l_ + 64, :], stab[l_:l_ + 32, t_], ALU.mult))
                em.op("vector", lambda v, b=bank, t_=ts_, l_=l: v.tensor_tensor(
                    rt2[l_ + 32:l_ + 64, :], b[l_:l_ + 32, :], stab[l_ + 32:l_ + 64, t_], ALU.mult))
            t_rope[mn] = em.op("vector", lambda v, d=dst, m_=m, t_=ts_: v.tensor_tensor(
                d[:, m_, t_], rt1[:, :], rt2[:, :], ALU.add))

    proj_qk(qT, d_wqT, d_Cq, d_Sq)
    proj_qk(kT, d_wkT, d_Ck, d_Sk)

    t_vev = {}
    for n in range(2):
        em.wait("gpsimd", em.now("tensor"))
        t_wv = em.dma("dgp", lambda g, n_=n: g.dma_start(
            wbuf[:], d_wvT[:, n_ * 512:(n_ + 1) * 512].rearrange("(k p) e -> p k e", p=128)))
        em.wait("tensor", t_wv)
        for m in range(16):
            mn = n * 16 + m
            bank = pb[4 + mn % 2]
            em.wait("tensor", t_vev.get(mn - 2))
            for kk in range(8):
                t_mm = em.op("tensor", lambda te, b=bank, m_=m, k=kk: te.matmul(
                    b[:, :], haT[:, k, m_ * 128:(m_ + 1) * 128],
                    wbuf[:, k, :], start=(k == 0), stop=(k == 7)))
            em.wait("scalar", t_mm)
            t_vev[mn] = em.op("scalar", lambda s, b=bank, m_=m, n_=n: s.copy(
                vtok[:, m_, n_ * 512:(n_ + 1) * 512], b[:, :]))

    if "dbg_qT" in dbg:
        em.fence()
        em.dma("dsync", lambda s: s.dma_start(dbg_outs["dbg_qT"], qT[:]))
        em.dma("dsync", lambda s: s.dma_start(dbg_outs["dbg_kT"], kT[:]))
    if "dbg_v" in dbg:
        em.fence()
        em.dma("dsync", lambda s: s.dma_start(dbg_outs["dbg_v"], vtok[:]))

    rt2_cm.__exit__(None, None, None)
    rt1_cm.__exit__(None, None, None)
    stab_cm.__exit__(None, None, None)
    ctab_cm.__exit__(None, None, None)
    wbuf_cm.__exit__(None, None, None)
    haT_cm.__exit__(None, None, None)

    em.fence()
    if stop_after == 2:
        return _early_finish()

    quant_jobs = []
    t_qst1 = {}
    t_qst2 = {}
    _qslot = [0]
    _qprev = {}

    QB = {"wst": wst, "tqb": tqb, "slots": 2, "prefetch": False}
    _wfree_ve = {}
    _wfree_se = {}

    def _quant_unit(srcap, dstap, cols, done_cb, half2_src=None, half2_dst=None):
        def job():
            wst_, tqb_, S = QB["wst"], QB["tqb"], QB["slots"]
            sl = _qslot[0] % S
            _qslot[0] += 1
            em.wait("gpsimd", _wfree_ve.get((S, sl)))
            em.wait("gpsimd", _wfree_se.get((S, sl)))
            t_ld = em.dma("dgp", lambda g: g.dma_start(wst_[:, sl, :], srcap))
            sl2 = None
            if half2_src is not None:
                sl2 = _qslot[0] % S
                _qslot[0] += 1
                em.wait("gpsimd", _wfree_ve.get((S, sl2)))
                em.wait("gpsimd", _wfree_se.get((S, sl2)))
                t_ld2 = em.dma("dgp", lambda g: g.dma_start(wst_[:, sl2, :], half2_src))
            yield
            em.wait("scalar", t_ld)
            em.wait("scalar", _qprev.get((S, sl)))
            t_abs = em.op("scalar", lambda s, s1=sl: s.activation(
                tqb_[:, s1, :], wst_[:, s1, :], AF.Abs, accum_out=stat[:, 22:23]))
            _wfree_se[(S, sl)] = t_abs
            if sl2 is not None:
                em.wait("scalar", t_ld2)
                em.wait("scalar", _qprev.get((S, sl2)))
                t_abs = em.op("scalar", lambda s, s1=sl2: s.activation(
                    tqb_[:, s1, :], wst_[:, s1, :], AF.Abs, accum_out=stat[:, 21:22]))
                _wfree_se[(S, sl2)] = t_abs
                em.wait("vector", t_abs)
                em.vsafe()
                em.op("vector", lambda v: v.tensor_tensor(stat[:, 22:23], stat[:, 22:23],
                                                          stat[:, 21:22], ALU.add))
            em.wait("vector", t_abs)
            em.vsafe()
            em.op("vector", lambda v: v.tensor_scalar(
                stat[:, 23:24], stat[:, 22:23], 1.0 / cols, 1e-5, ALU.mult, ALU.max))
            em.vsafe()
            em.op("vector", lambda v: v.tensor_scalar(stat[:, 26:27], stat[:, 23:24], 0.5, None, ALU.mult, ALU.bypass))
            em.vsafe()
            em.op("vector", lambda v: v.tensor_scalar(stat[:, 27:28], stat[:, 26:27], -1.0, None, ALU.mult, ALU.bypass))
            em.op("vector", lambda v: v.tensor_scalar(stat[:, 20:21], stat[:, 23:24], -1.0, None, ALU.mult, ALU.bypass))
            em.vsafe()
            t_st = None
            pairs = [(sl, dstap)] if half2_src is None else [(sl, dstap), (sl2, half2_dst)]
            for s_, d_ in pairs:
                em.op("vector", lambda v, s2=s_: v.tensor_scalar(
                    geb[:, :], wst_[:, s2, :], stat[:, 26:27], stat[:, 20:21], ALU.is_ge, ALU.mult))
                em.wait("vector", _qprev.get((S, s_)))
                em.op("vector", lambda v, s2=s_: v.tensor_scalar(
                    tqb_[:, s2, :], wst_[:, s2, :], stat[:, 27:28], stat[:, 20:21], ALU.is_le, ALU.mult))
                t_tq = em.op("vector", lambda v, s2=s_: v.tensor_tensor(
                    tqb_[:, s2, :], tqb_[:, s2, :], geb[:, :], ALU.subtract))
                _wfree_ve[(S, s_)] = t_tq
                em.wait("gpsimd", t_tq)
                t_st = em.dma("dgp", lambda g, s2=s_, dd=d_: g.dma_start(dd, tqb_[:, s2, :]))
                _qprev[(S, s_)] = t_st
            done_cb(t_st)
        job.nslots = 1 if half2_src is None else 2
        return job

    for mat9, (src1, src2) in enumerate(
            [(d_sw1, d_sw2)] + [(d_ew1[e], d_ew2[e]) for e in range(NEXP)]):
        for ch in range(16):
            quant_jobs.append(_quant_unit(
                src1[ch * 128:(ch + 1) * 128, :], d_q1[mat9, ch * 128:(ch + 1) * 128, :], E,
                lambda t, m=mat9: t_qst1.__setitem__(m, t)))
        for ch in range(8):
            quant_jobs.append(_quant_unit(
                src2[ch * 128:(ch + 1) * 128, 0:E], d_q2[mat9, ch * 128:(ch + 1) * 128, 0:E], FF,
                lambda t, m=mat9: t_qst2.__setitem__(m, t),
                half2_src=src2[ch * 128:(ch + 1) * 128, E:FF],
                half2_dst=d_q2[mat9, ch * 128:(ch + 1) * 128, E:FF]))
    quant_jobs.reverse()

    _qpop = [0]
    _primed = []
    _qfree = [2]

    def pop_quant(n=1):
        for _ in range(n):
            if not (quant_jobs or _primed):
                return
            _qpop[0] += 1
            if _qpop[0] > _QMAX[0]:
                if quant_jobs:
                    quant_jobs.pop()
                continue
            if QB["prefetch"]:
                while len(_primed) < 2 and quant_jobs and quant_jobs[-1].nslots <= _qfree[0]:
                    fn = quant_jobs.pop()
                    g = fn()
                    next(g)
                    _primed.append((g, fn.nslots))
                    _qfree[0] -= fn.nslots
                if _primed:
                    g, ns = _primed.pop(0)
                    for _x in g:
                        pass
                    _qfree[0] += ns
                elif quant_jobs:
                    for _x in quant_jobs.pop()():
                        pass
            else:
                if quant_jobs:
                    for _x in quant_jobs.pop()():
                        pass

    XE00, XE00_cm = sbuf("XE00", [128, T], BF16)
    XE10, XE10_cm = sbuf("XE10", [128, T], BF16)
    XE11, XE11_cm = sbuf("XE11", [128, T], BF16)
    XEp = [[XE00, XE00], [XE10, XE11]]
    Dbuf, Dbuf_cm = sbuf("Dbuf", [128, T], BF16)
    DTf, DTf_cm = sbuf("DTf", [128, 16, 4, 128], BF16)
    Abf, Abf_cm = sbuf("Abf", [128, 512], BF16)
    A2bf, A2bf_cm = sbuf("A2bf", [128, 512], BF16)
    gBsb, gBsb_cm = sbuf("gBsb", [128, 512], BF16)
    attnT, attnT_cm = sbuf("attnT", [128, 8, 512], BF16)
    wo8, wo8_cm = sbuf("wo8", [128, 8, 512], BF16)
    wotmp, wotmp_cm = sbuf("wotmp", [128, 512], BF16)

    t_sbank = {j: None for j in range(8)}
    t_dT = {}
    t_pv_of = {}
    pv_cells = {}
    t_attnT = {}
    t_wo_mm = {}
    st_prev = {"acopy": None, "s2": None, "d": None, "dT": None}

    t_expw = {}
    prev_n = [None]

    def emit_chunk_front(n):
        pg, i = divmod(n, 4)
        Q, p = divmod(pg, 8)
        tchunk = (pg // 8) * 4 + i
        par = n % 2
        mxc = 10 + 2 * par
        ngc = 8 if par == 0 else 14
        smc = 16 + 2 * par

        nm1 = prev_n[0]
        t_25 = None
        if nm1 is not None:
            pm1 = nm1 % 2
            em.wait("vector", t_expw[(nm1, 1)])
            em.op("vector", lambda v, sc=16 + 2 * pm1: v.reciprocal(stat[:, 24:25], stat[:, sc:sc + 1]))
            em.op("vector", lambda v, sc=17 + 2 * pm1: v.reciprocal(stat[:, 30:31], stat[:, sc:sc + 1]))
            em.vsafe()
            t_25 = em.op("vector", lambda v: v.tensor_scalar(
                stat[:, 25:26], stat[:, 30:31], lamneg[:, 0:1], None, ALU.mult, ALU.bypass))

        t_mm_h = {}
        for h in range(2):
            for c in range(4):
                bid = h * 4 + c
                em.wait("tensor", t_sbank[bid])
                base = 0 if h == 0 else 64
                tp = None if h == 0 else (64, 0)
                t_mm_h[h] = em.op("tensor", lambda te, b=pb[bid], p_=p, c_=c, tc=tchunk, ba=base, tp_=tp: te.matmul(
                    b[:, :], qT[ba:ba + 64, p_, tc * 128:(tc + 1) * 128],
                    kT[ba:ba + 64, p_, c_ * 512:(c_ + 1) * 512],
                    start=True, stop=True, tile_position=tp_))

        t_d1 = None
        if nm1 is not None:
            pm1 = nm1 % 2
            if st_prev["dT"] is not None:
                em.wait("scalar", st_prev["dT"])
            em.wait("scalar", t_25)
            t_d1 = em.op("scalar", lambda s, X=XEp[0][pm1]: s.activation(
                Dbuf[:, :], X[:, :], AF.Copy, scale=stat[:, 24:25]))

        for h in range(2):
            em.wait("vector", t_mm_h[h])
            em.op("vector", lambda v, P=pbw[h], mc=mxc + h: v.reduce_max(
                stat[:, mc:mc + 1], P[:, :], AX.X))
        em.vsafe()
        t_ng = em.op("vector", lambda v, m0=mxc, n0=ngc: v.tensor_scalar(
            stat[:, n0:n0 + 2], stat[:, m0:m0 + 2], -1.0, None, ALU.mult, ALU.bypass))

        if nm1 is not None:
            pm1 = nm1 % 2
            em.wait("vector", t_d1)
            t_dp = em.op("vector", lambda v, X=XEp[1][pm1]: v.scalar_tensor_tensor(
                Dbuf[:, :], X[:, :], stat[:, 25:26], Dbuf[:, :], ALU.mult, ALU.add))
            em.wait("sync", t_dp)
            pgm1, im1 = divmod(nm1, 4)
            if pgm1 >= 1:
                if t_pv_of.get(pgm1 - 1) is not None:
                    em.wait("sync", t_pv_of[pgm1 - 1])
                else:
                    cell = {}
                    pv_cells.setdefault(pgm1 - 1, []).append(cell)
                    em.wait_deferred("sync", cell)
            t = em.dma("dsync", lambda s, i_=im1: s.dma_start_transpose(
                DTf[:, :, i_, :], Dbuf[:]))
            t_dT[pgm1] = t
            st_prev["dT"] = t

        em.wait("scalar", t_ng)
        for h in range(2):
            t_e = em.op("scalar", lambda s, P=pbw[h], X=XEp[h][par], b0=ngc + h, a0=smc + h: s.activation(
                X[:, :], P[:, :], AF.Exp, bias=stat[:, b0:b0 + 1], accum_out=stat[:, a0:a0 + 1]))
            t_expw[(n, h)] = t_e
            for c in range(4):
                t_sbank[h * 4 + c] = t_e
        prev_n[0] = n

    def emit_chunk_tail_final():
        nm1 = prev_n[0]
        pm1 = nm1 % 2
        em.wait("vector", t_expw[(nm1, 1)])
        em.op("vector", lambda v, sc=16 + 2 * pm1: v.reciprocal(stat[:, 24:25], stat[:, sc:sc + 1]))
        em.op("vector", lambda v, sc=17 + 2 * pm1: v.reciprocal(stat[:, 30:31], stat[:, sc:sc + 1]))
        em.vsafe()
        t_25 = em.op("vector", lambda v: v.tensor_scalar(
            stat[:, 25:26], stat[:, 30:31], lamneg[:, 0:1], None, ALU.mult, ALU.bypass))
        if st_prev["dT"] is not None:
            em.wait("scalar", st_prev["dT"])
        em.wait("scalar", t_25)
        t_d1 = em.op("scalar", lambda s, X=XEp[0][pm1]: s.activation(
            Dbuf[:, :], X[:, :], AF.Copy, scale=stat[:, 24:25]))
        em.wait("vector", t_d1)
        t_dp = em.op("vector", lambda v, X=XEp[1][pm1]: v.scalar_tensor_tensor(
            Dbuf[:, :], X[:, :], stat[:, 25:26], Dbuf[:, :], ALU.mult, ALU.add))
        em.wait("sync", t_dp)
        pgm1, im1 = divmod(nm1, 4)
        if t_pv_of.get(pgm1 - 1) is not None:
            em.wait("sync", t_pv_of[pgm1 - 1])
        t = em.dma("dsync", lambda s, i_=im1: s.dma_start_transpose(
            DTf[:, :, i_, :], Dbuf[:]))
        t_dT[pgm1] = t
        st_prev["dT"] = t

    def emit_pv_epi(pg):
        Q, p = divmod(pg, 8)
        em.wait("tensor", t_dT[pg])
        em.wait("tensor", t_sbank[0])
        if t_attnT.get(pg - 1) is not None:
            em.wait("scalar", t_attnT.get(pg - 1))
        t_pv_l = None
        for kk in range(16):
            t_pv_l = em.op("tensor", lambda te, k=kk, p_=p: te.matmul(
                pb[0][:, :], vtok[:, k, p_ * 128:(p_ + 1) * 128],
                DTf[:, k, :, :].rearrange("p i t -> p (i t)"), start=(k == 0), stop=(k == 15)))
        t_pv_of[pg] = t_pv_l
        for cell in pv_cells.pop(pg, []):
            cell["tok"] = t_pv_l
        em.wait("scalar", t_pv_l)
        t_ac = em.op("scalar", lambda s: s.copy(Abf[:], pb[0][:, :]))
        t_sbank[0] = t_ac
        em.wait("vector", t_ac)
        if st_prev["s2"] is not None:
            em.wait("vector", st_prev["s2"])
        t_a2 = em.op("vector", lambda v: v.tensor_tensor(A2bf[:], Abf[:], Abf[:], ALU.mult))
        em.wait("tensor", t_a2)
        em.wait("tensor", t_sbank[1])
        t_s2 = em.op("tensor", lambda te: te.matmul(pb[1][0:1, :], ones_l[:], A2bf[:], start=True, stop=True))
        st_prev["s2"] = t_s2
        em.wait("scalar", t_s2)
        em.op("scalar", lambda s: s.activation(grow[0:1, :], pb[1][0:1, :], AF.Ln, scale=1.0 / 128, bias=epsc[0:1, 0:1]))
        t_g = em.op("scalar", lambda s: s.activation(grow[0:1, :], grow[0:1, :], AF.Exp, scale=-0.5, bias=lgc[0:1, 0:1]))
        t_sbank[1] = t_g
        em.wait("vector", t_g)
        t_gbf = em.op("vector", lambda v: v.tensor_copy(growbf[0:1, :], grow[0:1, :]))
        em.wait("tensor", t_gbf)
        em.wait("tensor", t_sbank[2])
        t_gb = em.op("tensor", lambda te: te.matmul(pb[2][:, :], ones_c[:], growbf[0:1, :], start=True, stop=True))
        em.wait("scalar", t_gb)
        t_gc = em.op("scalar", lambda s: s.copy(gBsb[:], pb[2][:, :]))
        t_sbank[2] = t_gc
        em.wait("vector", t_gc)
        if Q >= 1:
            em.wait("vector", t_wo_mm.get(Q - 1))
        t_at = em.op("vector", lambda v, p_=p: v.tensor_tensor(
            attnT[:, p_, :], Abf[:], gBsb[:], ALU.mult))
        t_attnT[pg] = t_at

    t_wo_ld = {}
    t_wo_prevmm = [None]

    def emit_wo(Q):
        em.wait("tensor", t_attnT[Q * 8 + 7])
        t_res = {}
        t_mm = None
        for n in range(2):
            if (Q, n) not in t_wo_ld:
                if t_mm is not None:
                    em.wait("gpsimd", t_mm)
                elif Q > 0:
                    em.wait("gpsimd", t_wo_prevmm[0])
                t_wo_ld[(Q, n)] = em.dma("dgp", lambda g, n_=n: g.dma_start(
                    wo8[:], d_woT[:, n_ * 512:(n_ + 1) * 512].rearrange("(k p) e -> p k e", p=128)))
            em.wait("tensor", t_wo_ld[(Q, n)])
            for m in range(4):
                mn = m * 2 + n if False else n * 4 + m
                tchunk = Q * 4 + m
                bid = 3 + mn % 4
                em.wait("tensor", t_sbank[bid])
                em.wait("tensor", t_res.get(mn - 4))
                for kk in range(8):
                    t_mm = em.op("tensor", lambda te, b=pb[bid], k=kk, m_=m: te.matmul(
                        b[:, :], attnT[:, k, m_ * 128:(m_ + 1) * 128],
                        wo8[:, k, :], start=(k == 0), stop=(k == 7)))
                em.wait("vector", t_mm)
                em.op("vector", lambda v, b=pb[bid], n_=n: v.tensor_tensor(
                    wotmp[:, :], b[:, :], gaB[:, n_ * 512:(n_ + 1) * 512], ALU.mult))
                t_res[mn] = em.op("vector", lambda v, tc=tchunk, n_=n: v.tensor_tensor(
                    x_res[:, tc, n_ * 512:(n_ + 1) * 512], x_res[:, tc, n_ * 512:(n_ + 1) * 512],
                    wotmp[:, :], ALU.add))
                t_sbank[bid] = t_res[mn]
        t_wo_prevmm[0] = t_mm
        if "dbg_attnT" in dbg:
            em.wait("sync", t_attnT[Q * 8 + 7])
            t_dmp = em.dma("dsync", lambda s, Q_=Q: s.dma_start(dbg_outs["dbg_attnT"][Q_], attnT[:]))
            em.wait("vector", t_dmp)
        t_wo_mm[Q] = t_mm

    for pg in range(32):
        emit_chunk_front(pg * 4 + 0)
        emit_chunk_front(pg * 4 + 1)
        if pg > 0:
            emit_pv_epi(pg - 1)
        if pg % 8 == 0 and pg > 0:
            emit_wo(pg // 8 - 1)
        pop_quant(1)
        emit_chunk_front(pg * 4 + 2)
        emit_chunk_front(pg * 4 + 3)
        pop_quant(1)
    emit_chunk_tail_final()
    emit_pv_epi(31)
    emit_wo(3)

    t_x2 = em.now("vector")
    if "dbg_x2" in dbg:
        em.wait("sync", t_x2)
        em.dma("dsync", lambda s: s.dma_start(dbg_outs["dbg_x2"], x_res[:]))

    for cm in (wotmp_cm, wo8_cm, attnT_cm, gBsb_cm, A2bf_cm, Abf_cm, DTf_cm,
               Dbuf_cm, XE11_cm, XE10_cm, XE00_cm, vtok_cm, kT_cm, qT_cm):
        cm.__exit__(None, None, None)

    em.fence()
    if stop_after == 3:
        return _early_finish()

    hiT, hiT_cm = sbuf("hiT", [128, 8, T], BF16)
    rw_sb, rw_sb_cm = sbuf("rw_sb", [128, 8, 8], BF16)
    rjunk, _ = sbuf("rjunk", [128, 8], BF16)
    zb2, zb2_cm = sbuf("zb2", [128, E], F32)
    logit = mg
    t_rw = em.dma("dgp", lambda g: g.dma_start(rw_sb[:], d_rwT.rearrange("(k p) e -> p k e", p=128)))
    t_hic = {}
    bank_ev = {4: None, 5: None, 6: None, 7: None}
    rt_prev = [None]

    def emit_ln2(i):
        em.op("vector", lambda v, i_=i: v.reduce_sum(stat[:, 3:4], x_res[:, i_, :], AX.X))
        em.op("vector", lambda v, i_=i: v.scalar_tensor_tensor(
            geb[:, :], x_res[:, i_, :], 1.0, x_res[:, i_, :], ALU.mult, ALU.mult, accum_out=stat[:, 4:5]))
        em.vsafe()
        em.op("vector", lambda v: v.tensor_scalar(stat[:, 0:1], stat[:, 3:4], 1.0 / E, None, ALU.mult, ALU.bypass))
        em.vsafe()
        em.op("vector", lambda v: v.tensor_tensor(stat[:, 6:7], stat[:, 0:1], stat[:, 0:1], ALU.mult))
        em.op("vector", lambda v: v.tensor_scalar(stat[:, 5:6], stat[:, 4:5], 1.0 / E, None, ALU.mult, ALU.bypass))
        em.vsafe()
        t = em.op("vector", lambda v: v.tensor_scalar(stat[:, 1:2], stat[:, 5:6], stat[:, 6:7], 1e-5, ALU.subtract, ALU.add))
        em.wait("scalar", t)
        em.op("scalar", lambda s: s.activation(stat[:, 2:3], stat[:, 1:2], AF.Ln))
        t = em.op("scalar", lambda s: s.activation(stat[:, 2:3], stat[:, 2:3], AF.Exp, scale=-0.5))
        em.wait("vector", t)
        em.op("vector", lambda v, i_=i: v.tensor_scalar(
            zb2[:, :], x_res[:, i_, :], stat[:, 0:1], stat[:, 2:3], ALU.subtract, ALU.mult))
        em.op("vector", lambda v: v.scalar_tensor_tensor(zb2[:, :], zb2[:, :], 1.0, cb1B[:, :], ALU.mult, ALU.mult))
        em.op("vector", lambda v: v.tensor_tensor(zb2[:, :], zb2[:, :], sbB[:, :], ALU.add))
        em.op("vector", lambda v: v.reduce_sum(stat[:, 3:4], zb2[:, :], AX.X))
        em.op("vector", lambda v: v.scalar_tensor_tensor(
            geb[:, :], zb2[:, :], 1.0, zb2[:, :], ALU.mult, ALU.mult, accum_out=stat[:, 4:5]))
        em.vsafe()
        em.op("vector", lambda v: v.tensor_scalar(stat[:, 0:1], stat[:, 3:4], 1.0 / E, None, ALU.mult, ALU.bypass))
        em.vsafe()
        em.op("vector", lambda v: v.tensor_tensor(stat[:, 6:7], stat[:, 0:1], stat[:, 0:1], ALU.mult))
        em.op("vector", lambda v: v.tensor_scalar(stat[:, 5:6], stat[:, 4:5], 1.0 / E, None, ALU.mult, ALU.bypass))
        em.vsafe()
        t = em.op("vector", lambda v: v.tensor_scalar(stat[:, 1:2], stat[:, 5:6], stat[:, 6:7], 1e-5, ALU.subtract, ALU.add))
        em.wait("scalar", t)
        em.op("scalar", lambda s: s.activation(stat[:, 2:3], stat[:, 1:2], AF.Ln))
        t = em.op("scalar", lambda s: s.activation(stat[:, 2:3], stat[:, 2:3], AF.Exp, scale=-0.5))
        em.wait("vector", t)
        t_z2 = em.op("vector", lambda v: v.tensor_scalar(
            zb2[:, :], zb2[:, :], stat[:, 0:1], stat[:, 2:3], ALU.subtract, ALU.mult))
        em.wait("tensor", t_z2)
        t_ev = None
        for j in range(8):
            bk = 4 + (j % 4)
            em.wait("tensor", bank_ev[bk])
            t_t = em.op("tensor", lambda te, j_=j, b=pb[bk]: te.transpose(
                b[:, 0:128], zb2[:, j_ * 128:(j_ + 1) * 128], idf[:]))
            em.wait("vector", t_t)
            t_ev = em.op("vector", lambda v, i_=i, j_=j, b=pb[bk]: v.tensor_scalar(
                hiT[:, j_, i_ * 128:(i_ + 1) * 128], b[:, 0:128],
                gamc[:, j_:j_ + 1], betc[:, j_:j_ + 1], ALU.mult, ALU.add))
            bank_ev[bk] = t_ev
        t_hic[i] = t_ev

    def emit_router(i):
        em.wait("tensor", t_rw)
        em.wait("tensor", bank_ev[7])
        em.wait("tensor", rt_prev[0])
        for kk in range(8):
            t_mm = em.op("tensor", lambda te, k=kk, i_=i: te.matmul(
                pb[7][:, 0:8], hiT[:, k, i_ * 128:(i_ + 1) * 128], rw_sb[:, k, :],
                start=(k == 0), stop=(k == 7)))
        em.wait("vector", t_mm)
        t_lc = em.op("vector", lambda v: v.tensor_copy(logit[:], pb[7][:, 0:8]))
        bank_ev[7] = t_lc
        rt_prev[0] = t_lc
        em.vsafe()
        em.op("vector", lambda v: v.reduce_max(stat[:, 0:1], logit[:], AX.X))
        em.vsafe()
        t_nm = em.op("vector", lambda v: v.tensor_scalar(stat[:, 1:2], stat[:, 0:1], -1.0, None, ALU.mult, ALU.bypass))
        em.wait("scalar", t_nm)
        t_e8 = em.op("scalar", lambda s: s.activation(rjunk[:, :], logit[:], AF.Exp,
                                                      bias=stat[:, 1:2], accum_out=stat[:, 2:3]))
        em.wait("vector", t_e8)
        em.op("vector", lambda v: v.reciprocal(stat[:, 3:4], stat[:, 2:3]))
        em.vsafe()
        em.op("vector", lambda v, i_=i: v.tensor_scalar(
            gates[:, i_, :], logit[:], stat[:, 0:1], stat[:, 3:4], ALU.is_equal, ALU.mult))

    QB["prefetch"] = True
    w1q, w1q_cm = sbuf("w1q", [128, 8, FF], BF16)
    w2q, w2q_cm = sbuf("w2q", [128, 16, E], BF16)
    h1T, h1T_cm = sbuf("h1T", [128, 16, 512], BF16)
    racc, racc_cm = sbuf("racc", [128, 16, E], BF16)

    for i in range(4):
        emit_ln2(i)

    t_rac = {}
    t_lastmm1 = None
    t_lastmm2 = None
    for mat in range(9):
        if 60 <= stop_after < 69 and mat > stop_after - 60:
            break
        while len(quant_jobs) + len(_primed) > 24 * (8 - mat):
            pop_quant(1)
        em.wait("sync", t_qst1.get(mat))
        em.wait("sync", t_qst2.get(mat))
        if t_lastmm1 is not None:
            em.wait("sync", t_lastmm1)
        t_w1r = em.dma("dsync", lambda s, m=mat: s.dma_start_transpose(w1q[:], d_q1[m]))
        if t_lastmm2 is not None:
            em.wait("sync", t_lastmm2)
        t_w2r = em.dma("dsync", lambda s, m=mat: s.dma_start_transpose(w2q[:], d_q2[m]))
        if "dbg_w1q" in dbg and mat == 1:
            em.wait("sync", t_w2r)
            em.dma("dsync", lambda s: s.dma_start(dbg_outs["dbg_w1q"], w1q[:]))
            em.dma("dsync", lambda s: s.dma_start(dbg_outs["dbg_w2q"], w2q[:]))
        em.wait("tensor", t_w1r)
        t_sil = {}
        t_mm2_last_q = None
        for Q in range(4):
            if t_mm2_last_q is not None:
                em.wait("scalar", t_mm2_last_q)
            if mat == 0:
                for tc in range(4 * Q, 4 * Q + 4):
                    em.wait("tensor", t_hic[tc])
            for m in range(16):
                bank = pb[m % 2]
                em.wait("tensor", t_sil.get(m - 2))
                t_mm = None
                for kk in range(8):
                    t_mm = em.op("tensor", lambda te, b=bank, m_=m, k=kk, Q_=Q: te.matmul(
                        b[:, :], w1q[:, k, m_ * 128:(m_ + 1) * 128],
                        hiT[:, k, Q_ * 512:(Q_ + 1) * 512], start=(k == 0), stop=(k == 7)))
                em.wait("scalar", t_mm)
                t_sil[m] = em.op("scalar", lambda s, b=bank, m_=m: s.activation(
                    h1T[:, m_, :], b[:, :], AF.Silu))
                if m % 4 == 1:
                    pop_quant(1)
                if mat == 0 and Q < 3 and m % 4 == 3:
                    emit_ln2(4 * (Q + 1) + m // 4)
            t_lastmm1 = t_mm
            em.wait("tensor", t_sil[15])
            em.wait("tensor", t_w2r)
            for mt in range(4):
                tchunk = Q * 4 + mt
                for n in range(2):
                    bank = pb[2 + (mt * 2 + n) % 2]
                    em.wait("tensor", t_rac.get((tchunk, n)))
                    t_mm2 = None
                    for kk in range(16):
                        t_mm2 = em.op("tensor", lambda te, b=bank, k=kk, mt_=mt, n_=n: te.matmul(
                            b[:, :], h1T[:, k, mt_ * 128:(mt_ + 1) * 128],
                            w2q[:, k, n_ * 512:(n_ + 1) * 512], start=(k == 0), stop=(k == 15)))
                    t_mm2_last_q = t_mm2
                    if mat == 0:
                        em.wait("scalar", t_mm2)
                        t_rac[(tchunk, n)] = em.op("scalar", lambda s, b=bank, tc=tchunk, n_=n: s.copy(
                            racc[:, tc, n_ * 512:(n_ + 1) * 512], b[:, :]))
                    else:
                        em.wait("vector", t_mm2)
                        em.wait("vector", t_rac[(tchunk, n)])
                        t_rac[(tchunk, n)] = em.op("vector", lambda v, b=bank, tc=tchunk, n_=n, m_=mat: v.scalar_tensor_tensor(
                            racc[:, tc, n_ * 512:(n_ + 1) * 512], b[:, :], gates[:, tc, m_ - 1:m_],
                            racc[:, tc, n_ * 512:(n_ + 1) * 512], ALU.mult, ALU.add))
            t_lastmm2 = t_mm2
            if mat == 0:
                for tc in range(4 * Q, 4 * Q + 4):
                    emit_router(tc)
        if mat == 0:
            if "dbg_hiT" in dbg:
                em.wait("sync", t_hic[15])
                em.dma("dsync", lambda s: s.dma_start(dbg_outs["dbg_hiT"], hiT[:]))
            if "dbg_gate" in dbg:
                em.wait("sync", em.now("vector"))
                em.dma("dsync", lambda s: s.dma_start(dbg_outs["dbg_gate"], gates[:]))

    em.fence()

    t_ost = {}
    obufs = [zb2[:, :], wst[:, 0, :], wst[:, 1, :]]
    for i in range(16):
        ob = obufs[i % 3]
        if i >= 3:
            em.wait("vector", t_ost[i - 3])
        em.op("vector", lambda v, i_=i: v.scalar_tensor_tensor(
            racc[:, i_, :], racc[:, i_, :], 1.0, gbB[:, :], ALU.mult, ALU.mult))
        t_fin = em.op("vector", lambda v, i_=i, b=ob: v.tensor_tensor(
            b, x_res[:, i_, :], racc[:, i_, :], ALU.add))
        em.wait("sync", t_fin)
        t_ost[i] = em.dma("dsync", lambda s, i_=i, b=ob: s.dma_start(
            d_out[i_ * 128:(i_ + 1) * 128, :], b))

    em.final_wait()

    with nc.Block() as block:
        em.replay(block)
    ctx.close()
    return nc


def _prep_inputs(inputs, core):
    f32c = lambda a: np.ascontiguousarray(a, dtype=np.float32)
    Cq, Sq, Ck, Sk = _rope_tables()
    return {
        "x": f32c(inputs["L0_attn_x"][core]),
        "wqT": f32c(np.asarray(inputs["L0_attn_Wq"]).T[:, _PERM_E]),
        "wkT": f32c(np.asarray(inputs["L0_attn_Wk"]).T[:, _PERM_E]),
        "wvT": f32c(np.asarray(inputs["L0_attn_Wv"]).T),
        "woT": f32c(np.asarray(inputs["L0_attn_Wo"]).T),
        "wmod": f32c(inputs["L0_wmod"]),
        "c8": f32c(np.asarray(inputs["L0_c"])[core].reshape(8, 128).T),
        "lamv": f32c(np.concatenate([np.asarray(inputs["L0_attn_lambda_q1"]), np.asarray(inputs["L0_attn_lambda_k1"]),
                                     np.asarray(inputs["L0_attn_lambda_q2"]), np.asarray(inputs["L0_attn_lambda_k2"])]).reshape(1, 256)),
        "rwT": f32c(np.asarray(inputs["L0_ffn_router_w"]).T),
        "gam": f32c(np.asarray(inputs["L0_ffn_ln_gamma"]).reshape(8, 128).T),
        "bet": f32c(np.asarray(inputs["L0_ffn_ln_beta"]).reshape(8, 128).T),
        "sw1": f32c(inputs["L0_ffn_shared_w1"]),
        "sw2": f32c(inputs["L0_ffn_shared_w2"]),
        "ew1": f32c(inputs["L0_ffn_expert_w1"]),
        "ew2": f32c(inputs["L0_ffn_expert_w2"]),
        "idb": np.eye(128, dtype=ml_dtypes.bfloat16),
        "idf": np.eye(128, dtype=np.float32),
        "Cq": Cq, "Sq": Sq, "Ck": Ck, "Sk": Sk,
    }


def kernel(**inputs):
    nc = build_program()
    in_maps = [_prep_inputs(inputs, core) for core in range(8)]
    res = run_bass_kernel_spmd(nc, in_maps, core_ids=list(range(8)))
    return np.stack([np.asarray(r["out"]).astype(np.float32) for r in res.results], axis=0)
